# version 09
# baseline (speedup 1.0000x reference)
import numpy as np
from contextlib import ExitStack

import concourse.bass as bass
import concourse.mybir as mybir
from concourse.bass_utils import run_bass_kernel_spmd

F32 = mybir.dt.float32
BF16 = mybir.dt.bfloat16
AF = mybir.ActivationFunctionType
ALU = mybir.AluOpType
AX = mybir.AxisListType

D = 1024
S = 4096
T = 2048
NP = 8
EPS = 1e-5
KT = S // 128
QC = T // 512
CW = 256
NPT = 4


class Ctx:
    def __init__(self, nc, stack):
        self.nc = nc
        self.stack = stack
        self.sems = {}
        self.hwm = {}

    def sem(self, name, step=1):
        h = self.stack.enter_context(self.nc.semaphore(name))
        self.sems[name] = [h, step, 0]

    def inc(self, name, inst):
        e = self.sems[name]
        inst.then_inc(e[0], e[1])
        e[2] += e[1]
        return e[2]

    def val(self, name):
        return self.sems[name][2]

    def wait(self, ekey, eng, name, v):
        if v <= 0:
            return
        key = (ekey, name)
        if self.hwm.get(key, 0) >= v:
            return
        eng.wait_ge(self.sems[name][0], v)
        self.hwm[key] = v

    def barrier(self, engines, exclude=()):
        for ekey, eng in engines:
            for name in self.sems:
                if name in exclude:
                    continue
                self.wait(ekey, eng, name, self.val(name))


class Rel:
    def __init__(self, ctx):
        self.ctx = ctx
        self.rel = []

    def acquire(self, ekey, eng):
        for (sn, vv) in self.rel:
            self.ctx.wait(ekey, eng, sn, vv)
        self.rel = []

    def release(self, sem_name, val):
        self.rel.append((sem_name, val))


def build(taps=()):
    nc = bass.Bass("TRN2", target_bir_lowering=False)
    dr = lambda n, sh, kind: nc.dram_tensor(n, sh, F32, kind=kind).ap()

    x_full = dr("x_full", [D, S], "ExternalInput")
    x_own = dr("x_own", [D, T], "ExternalInput")
    wq = dr("wq", [128, 8 * 1024], "ExternalInput")
    wk = dr("wk", [128, 8 * 64], "ExternalInput")
    wv = dr("wv", [128, 8 * 64], "ExternalInput")
    wr = dr("wr", [128, 8 * 7], "ExternalInput")
    gate_t = dr("gate_t", [8, 4, 8, 128, 1024], "ExternalInput")
    up_t = dr("up_t", [8, 4, 8, 128, 1024], "ExternalInput")
    down_t = dr("down_t", [8, 4, 8, 128, 1024], "ExternalInput")
    out = dr("out", [D, T], "ExternalOutput")
    gate_b = nc.dram_tensor("gate_b", [8, 4, 8, 128, 1024], BF16, kind="Internal").ap()
    up_b = nc.dram_tensor("up_b", [8, 4, 8, 128, 1024], BF16, kind="Internal").ap()
    down_b = nc.dram_tensor("down_b", [8, 4, 8, 128, 1024], BF16, kind="Internal").ap()

    tap_aps = {}

    with ExitStack() as stack:
        ctx = Ctx(nc, stack)
        SB = lambda n, sh, dt=F32: stack.enter_context(nc.sbuf_tensor(n, sh, dt)).ap()
        sync, v, s, t, g = nc.sync, nc.vector, nc.scalar, nc.tensor, nc.gpsimd
        Ts, Ss, Vs, Gs, Qs = ("t", t), ("s", s), ("v", v), ("g", g), ("q", sync)
        ALL_ENG = [Ts, Ss, Vs, Gs, Qs]

        for name, step in ([("qg", 16), ("qxr", 16), ("qwq", 16), ("qsw", 16), ("qpc", 16),
                            ("pe", 1), ("act", 1),
                            ("dve", 1), ("pool", 1), ("st", 16)]
                           + [(f"qw{i}", 16) for i in range(6)]
                           + [(f"qo{i}", 16) for i in range(3)]
                           + [(f"qxf{i}", 16) for i in range(2)]):
            ctx.sem(name, step)
        W = lambda ee, name, val: ctx.wait(ee[0], ee[1], name, val)

        def tap(name, shape, src_ap, dep_sem):
            if name not in taps:
                return
            ap = dr("tap_" + name, shape, "ExternalOutput")
            tap_aps[name] = ap
            W(Qs, dep_sem, ctx.val(dep_sem))
            ctx.inc("st", sync.dma_start(ap, src_ap))

        XR = SB("XR", [128, 8 * T])
        ln2 = SB("ln2", [128, 8 * T], BF16)
        comb = SB("comb", [7, T])
        lg = SB("lg", [7, 512])
        lgT = SB("lgT", [128, 4 * 7])
        st1 = SB("st1", [128, 4]); st2 = SB("st2", [128, 4])
        st3 = SB("st3", [128, 4]); st4 = SB("st4", [128, 4])
        pr = SB("pr", [128, 28]); eq1 = SB("eq1", [128, 28])
        eq2 = SB("eq2", [128, 28]); combT = SB("combT", [128, 28])
        PS = stack.enter_context(nc.psum_tensor("PS", [128, 4096], F32)).ap()
        bank = lambda i: PS[:, i * 512:(i + 1) * 512]
        brel = [Rel(ctx) for _ in range(8)]

        wk_b = SB("wk_b", [128, 8 * 64], BF16)
        wv_b = SB("wv_b", [128, 8 * 64], BF16)
        wr_b = SB("wr_b", [128, 8 * 7], BF16)
        ones_bf = SB("ones_bf", [128, 2], BF16)
        ones_f = SB("ones_f", [1, 128])
        eps_t = SB("eps_t", [1, 2])
        ident = SB("ident", [128, 128])
        srow = SB("srow", [1, 3 * 512])
        mrow = SB("mrow", [1, 16])
        arow = SB("arow", [1, 16])

        ctx.inc("dve", v.memset(ones_bf, 1.0))
        ctx.inc("dve", v.memset(ones_f, 1.0))
        consts_dve = ctx.inc("dve", v.memset(eps_t, EPS))
        ctx.inc("pool", g.memset(ident, 1.0))
        W(Gs, "pool", ctx.val("pool"))
        ident_done = ctx.inc("pool", g.affine_select(
            ident, ident, [[1, 128]], ALU.is_equal, 0.0, base=0, channel_multiplier=-1))

        ctx.inc("qsw", g.dma_start(wk_b, wk))
        ctx.inc("qsw", g.dma_start(wv_b, wv))
        w_small = ctx.inc("qsw", g.dma_start(wr_b, wr))
        wq_done = None

        xr_load = ctx.inc("qxr", sync.dma_start(
            XR.rearrange("p (c n) -> p c n", c=8),
            x_own.rearrange("(c p) n -> p c n", p=128)))

        with ExitStack() as fa:
            SBA = lambda n, sh, dt=F32: fa.enter_context(nc.sbuf_tensor(n, sh, dt)).ap()
            Q = SBA("Q", [128, 8 * T], BF16)
            K2 = SBA("K2", [128, S], BF16)
            VA = SBA("VA", [128, KT * 65 + 64], BF16)
            va_ones = ctx.inc("dve", v.memset(VA, 1.0))

            with ExitStack() as fa1:
                fa1.enter_context(nc.named_scope("P1"))
                SB1 = lambda n, sh, dt=F32: fa1.enter_context(nc.sbuf_tensor(n, sh, dt)).ap()
                wq_b = SB1("wq_b", [128, 8 * 1024], BF16)
                xf = [SB1(f"xf{i}", [128, 8 * CW]) for i in range(2)]
                xf_rel = [Rel(ctx) for _ in range(2)]
                xbt = [SB1(f"xbt{i}", [128, CW], BF16) for i in range(2)]
                xbt_rel = [Rel(ctx) for _ in range(2)]
                xst = [SB1(f"xst{i}", [128, CW], BF16) for i in range(2)]
                xst_rel = [Rel(ctx) for _ in range(2)]
                lnb = [SB1(f"lnb{i}", [128, 8 * CW], BF16) for i in range(2)]
                lnb_rel = [Rel(ctx) for _ in range(2)]
                nrm = [SB1(f"nrm{i}", [128, CW]) for i in range(2)]
                nrm_rel = [Rel(ctx) for _ in range(2)]

                wq_done = ctx.inc("qwq", g.dma_start(wq_b, wq))

                xr_src = x_full.rearrange("(c p) n -> p c n", p=128)
                for j in range(16):
                    bi = j % 2
                    own = j >= 8
                    xf_rel[bi].acquire(*Qs)
                    dv = ctx.inc(f"qxf{bi}", sync.dma_start(
                        xf[bi].rearrange("p (c n) -> p c n", c=8),
                        xr_src[:, :, j * CW:(j + 1) * CW]))
                    brel[0].acquire(*Ts)
                    W(Ts, "dve", consts_dve)
                    for c8 in range(8):
                        ti = c8 % 2
                        xbt_rel[ti].acquire(*Vs)
                        W(Vs, f"qxf{bi}", dv)
                        cv = ctx.inc("dve", v.tensor_copy(
                            xbt[ti], xf[bi][:, c8 * CW:(c8 + 1) * CW]))
                        W(Ts, "dve", cv)
                        mv = ctx.inc("pe", t.matmul(
                            bank(0)[0:1, 0:CW], ones_bf[:, 0:1], xbt[ti],
                            start=(c8 == 0), stop=(c8 == 7)))
                        xbt_rel[ti].release("pe", mv)
                    mv_sum = mv
                    brel[1].acquire(*Ts)
                    for c8 in range(8):
                        ti = c8 % 2
                        xst_rel[ti].acquire(*Ss)
                        W(Ss, f"qxf{bi}", dv)
                        sv = ctx.inc("act", s.activation(
                            xst[ti], xf[bi][:, c8 * CW:(c8 + 1) * CW], AF.Square))
                        W(Ts, "act", sv)
                        mq = ctx.inc("pe", t.matmul(
                            bank(1)[0:1, 0:CW], ones_bf[:, 0:1], xst[ti],
                            start=(c8 == 0), stop=(c8 == 7)))
                        xst_rel[ti].release("pe", mq)
                    mv_sq = mq
                    W(Vs, "pe", mv_sq)
                    dv1 = ctx.inc("dve", v.tensor_scalar_mul(
                        srow[0:1, 0:CW], bank(0)[0:1, 0:CW], 1.0 / D))
                    brel[0].release("dve", dv1)
                    dv2 = ctx.inc("dve", v.tensor_scalar_mul(
                        srow[0:1, 512:512 + CW], bank(1)[0:1, 0:CW], 1.0 / D))
                    W(Vs, "dve", dv2)
                    ctx.inc("dve", v.tensor_mul(srow[0:1, 1024:1024 + CW],
                                                srow[0:1, 0:CW], srow[0:1, 0:CW]))
                    W(Vs, "dve", ctx.val("dve"))
                    dvv = ctx.inc("dve", v.tensor_sub(
                        srow[0:1, 1024:1024 + CW], srow[0:1, 512:512 + CW],
                        srow[0:1, 1024:1024 + CW]))
                    W(Ss, "dve", dvv)
                    W(Ss, "dve", consts_dve)
                    ctx.inc("act", s.activation(srow[0:1, 512:512 + CW],
                                                srow[0:1, 1024:1024 + CW],
                                                AF.Ln, bias=eps_t[0:1, 0:1], scale=1.0))
                    W(Ss, "act", ctx.val("act"))
                    sv2 = ctx.inc("act", s.activation(srow[0:1, 512:512 + CW],
                                                      srow[0:1, 512:512 + CW],
                                                      AF.Exp, scale=-0.5))
                    W(Vs, "act", sv2)
                    dvn = ctx.inc("dve", v.scalar_tensor_tensor(
                        srow[0:1, 1024:1024 + CW], srow[0:1, 0:CW], -1.0,
                        srow[0:1, 512:512 + CW], ALU.mult, ALU.mult))
                    brel[2].acquire(*Ts)
                    W(Ts, "act", sv2)
                    W(Ts, "dve", dvn)
                    t.matmul(bank(1)[:, 0:CW], ones_f, srow[0:1, 512:512 + CW],
                             start=True, stop=True)
                    mvb = ctx.inc("pe", t.matmul(
                        bank(2)[:, 0:CW], ones_f, srow[0:1, 1024:1024 + CW],
                        start=True, stop=True))
                    lnb_rel[bi].acquire(*Vs)
                    W(Vs, "pe", mvb)
                    lv = 0
                    for c8 in range(8):
                        ni = c8 % 2
                        nrm_rel[ni].acquire(*Vs)
                        dvt = ctx.inc("dve", v.tensor_tensor(
                            nrm[ni], xf[bi][:, c8 * CW:(c8 + 1) * CW],
                            bank(1)[:, 0:CW], ALU.mult))
                        W(Vs, "dve", dvt)
                        lv = ctx.inc("dve", v.tensor_tensor(
                            lnb[bi][:, c8 * CW:(c8 + 1) * CW], nrm[ni],
                            bank(2)[:, 0:CW], ALU.add))
                        nrm_rel[ni].release("dve", lv)
                    xf_rel[bi].release("dve", lv)
                    brel[1].release("dve", lv)
                    brel[2].release("dve", lv)

                    brel[3].acquire(*Ts)
                    W(Ts, "qsw", w_small)
                    W(Ts, "dve", lv)
                    for c8 in range(8):
                        ins = t.matmul(bank(3)[0:64, 0:CW], wk_b[:, c8 * 64:(c8 + 1) * 64],
                                       lnb[bi][:, c8 * CW:(c8 + 1) * CW],
                                       start=(c8 == 0), stop=(c8 == 7))
                    mk = ctx.inc("pe", ins)
                    W(Ss, "pe", mk)
                    s.activation(K2[0:64, j * CW:(j + 1) * CW], bank(3)[0:64, 0:CW], AF.Copy)
                    se = ctx.inc("act", s.activation(K2[64:128, j * CW:(j + 1) * CW],
                                                     bank(3)[0:64, 0:CW], AF.Copy))
                    brel[3].release("act", se)
                    brel[4].acquire(*Ts)
                    for kt4 in range(CW // 128):
                        for c8 in range(8):
                            ins = t.matmul(
                                PS[:, 4 * 512 + kt4 * 64: 4 * 512 + (kt4 + 1) * 64],
                                lnb[bi][:, c8 * CW + kt4 * 128: c8 * CW + (kt4 + 1) * 128],
                                wv_b[:, c8 * 64:(c8 + 1) * 64],
                                start=(c8 == 0), stop=(c8 == 7))
                    mvv = ctx.inc("pe", ins)
                    W(Vs, "pe", mvv)
                    W(Vs, "dve", va_ones)
                    kt0 = j * (CW // 128)
                    dvv2 = ctx.inc("dve", v.tensor_copy(
                        VA[:, kt0 * 65:(kt0 + CW // 128) * 65]
                        .rearrange("p (k n) -> p k n", k=CW // 128)[:, :, 0:64],
                        PS[:, 4 * 512: 4 * 512 + (CW // 128) * 64]
                        .rearrange("p (k n) -> p k n", k=CW // 128)))
                    brel[4].release("dve", dvv2)
                    qlast = mvv
                    if own:
                        jj = j - 8
                        for hp in range(8):
                            bk = 5 + (hp % 2)
                            brel[bk].acquire(*Ts)
                            W(Ts, "qwq", wq_done)
                            for c8 in range(8):
                                ins = t.matmul(
                                    bank(bk)[:, 0:CW],
                                    wq_b[:, c8 * 1024 + hp * 128: c8 * 1024 + (hp + 1) * 128],
                                    lnb[bi][:, c8 * CW:(c8 + 1) * CW],
                                    start=(c8 == 0), stop=(c8 == 7))
                            mq2 = ctx.inc("pe", ins)
                            W(Ss, "pe", mq2)
                            qe = ctx.inc("act", s.activation(
                                Q[:, hp * T + jj * CW: hp * T + (jj + 1) * CW],
                                bank(bk)[:, 0:CW], AF.Copy))
                            brel[bk].release("act", qe)
                        qlast = ctx.val("pe")
                    lnb_rel[bi].release("pe", qlast)

                ctx.barrier(ALL_ENG, exclude=("qpc",))

            p1_act = ctx.val("act")
            p1_dve = ctx.val("dve")
            for e in range(8):
                for (srcw, dstw) in ((gate_t, gate_b), (up_t, up_b), (down_t, down_b)):
                    ctx.inc("qpc", g.dma_start(dstw[e], srcw[e]))
            qpc_all = ctx.val("qpc")
            tap("K2", [128, S // 2], K2.bitcast(F32), "act")
            tap("VA", [128, (KT * 65 + 1) // 2], VA.bitcast(F32), "dve")
            tap("Q", [128, 4 * T], Q.bitcast(F32), "act")

            with ExitStack() as fa2:
                fa2.enter_context(nc.named_scope("P2"))
                SB2 = lambda n, sh, dt=F32: fa2.enter_context(nc.sbuf_tensor(n, sh, dt)).ap()
                PT = [SB2(f"PT{i}", [128, 1024], BF16) for i in range(NPT)]
                PT_rel = [Rel(ctx) for _ in range(NPT)]
                att_s = SB2("att_s", [128, 512])
                bias_hd = SB2("bias_hd", [128, 8])
                rdrow2 = SB2("rdrow2", [1, 1024])
                rd_rel = Rel(ctx)
                xbt3 = [SB2(f"xbt3_{i}", [128, 256], BF16) for i in range(2)]
                xbt3_rel = [Rel(ctx) for _ in range(2)]
                xst3 = [SB2(f"xst3_{i}", [128, 256], BF16) for i in range(2)]
                xst3_rel = [Rel(ctx) for _ in range(2)]
                nrm3 = [SB2(f"nrm3_{i}", [128, 256]) for i in range(2)]
                nrm3_rel = [Rel(ctx) for _ in range(2)]

                with ExitStack() as fp:
                    SBP = lambda n, sh, dt=F32: fp.enter_context(nc.sbuf_tensor(n, sh, dt)).ap()
                    ksq = SBP("ksq", [64, S], BF16)
                    qsq = [SBP(f"qsq{i}", [128, T], BF16) for i in range(2)]
                    qsq_rel = [Rel(ctx) for _ in range(2)]

                    W(Ss, "act", p1_act)
                    kq = ctx.inc("act", s.activation(ksq, K2[0:64, :], AF.Square))
                    for c in range(8):
                        brel[7].acquire(*Ts)
                        W(Ts, "act", kq)
                        ins = t.matmul(bank(7)[0:1, :], ones_bf[0:64, 0:1],
                                       ksq[:, c * 512:(c + 1) * 512], start=True, stop=True)
                        mv = ctx.inc("pe", ins)
                        W(Vs, "pe", mv)
                        dv = ctx.inc("dve", v.reduce_max(mrow[0:1, c:c + 1],
                                                         bank(7)[0:1, :], axis=AX.X))
                        brel[7].release("dve", dv)
                    W(Vs, "dve", ctx.val("dve"))
                    ctx.inc("dve", v.reduce_max(mrow[0:1, 8:9], mrow[0:1, 0:8], axis=AX.X))

                    for hp in range(NP):
                        qi = hp % 2
                        qsq_rel[qi].acquire(*Ss)
                        W(Ss, "act", p1_act)
                        sv = ctx.inc("act", s.activation(qsq[qi], Q[:, hp * T:(hp + 1) * T],
                                                         AF.Square))
                        for qc in range(QC):
                            brel[7].acquire(*Ts)
                            W(Ts, "act", sv)
                            ins = t.matmul(bank(7)[0:1, :], ones_bf[:, 0:1],
                                           qsq[qi][:, qc * 512:(qc + 1) * 512],
                                           start=True, stop=True)
                            mv = ctx.inc("pe", ins)
                            W(Vs, "pe", mv)
                            dv = ctx.inc("dve", v.reduce_max(
                                mrow[0:1, 12 + qc:13 + qc], bank(7)[0:1, :], axis=AX.X))
                            brel[7].release("dve", dv)
                        qsq_rel[qi].release("pe", ctx.val("pe"))
                        W(Vs, "dve", ctx.val("dve"))
                        ctx.inc("dve", v.reduce_max(mrow[0:1, 10:11], mrow[0:1, 12:16],
                                                    axis=AX.X))
                        W(Vs, "dve", ctx.val("dve"))
                        ctx.inc("dve", v.tensor_add(arow[0:1, 0:1], mrow[0:1, 10:11],
                                                    mrow[0:1, 8:9]))
                        W(Vs, "dve", ctx.val("dve"))
                        dvb2 = ctx.inc("dve", v.tensor_scalar_mul(
                            arow[0:1, 0:1], arow[0:1, 0:1], -1.0 / 16.0))
                        brel[6].acquire(*Ts)
                        W(Ts, "dve", dvb2)
                        ins = t.matmul(PS[:, 6 * 512:6 * 512 + 1], ones_f,
                                       arow[0:1, 0:1], start=True, stop=True)
                        mv = ctx.inc("pe", ins)
                        W(Vs, "pe", mv)
                        dvc = ctx.inc("dve", v.tensor_copy(bias_hd[:, hp:hp + 1],
                                                           PS[:, 6 * 512:6 * 512 + 1]))
                        brel[6].release("dve", dvc)
                    bias_done = ctx.val("dve")
                    ctx.barrier(ALL_ENG, exclude=("qpc",))

                wg0 = [SB2(f"wg0_{i}", [128, 8 * 128], BF16) for i in range(2)]
                wg0_rel = [Rel(ctx) for _ in range(2)]
                wu0 = [SB2(f"wu0_{i}", [128, 8 * 128], BF16) for i in range(2)]
                wu0_rel = [Rel(ctx) for _ in range(2)]
                wd0 = [SB2(f"wd0_{i}", [128, 8 * 128], BF16) for i in range(2)]
                wd0_rel = [Rel(ctx) for _ in range(2)]
                hq = SB2("hq", [128, 8 * 512], BF16)
                hq_rel = [Rel(ctx) for _ in range(8)]
                silg1 = SB2("silg1", [128, 512])
                silg1b = SB2("silg1b", [128, 512])
                silg1_rel = Rel(ctx)

                xr_dve = [[0] * QC for _ in range(NP)]
                ln2_done = [0] * QC
                comb_done = [0] * QC
                p3_act = [0] * QC
                a2_prev = [0]

                xr_last0 = [[0] * QC for _ in range(8)]
                hq_done = [0] * 8
                chunks_done = [0]
                ov_items = []
                for qcf in range(QC):
                    for db in range(4):
                        ov_items += [("gu", db, pc, qcf) for pc in range(8)]
                        ov_items += [("dn", db, oc, qcf) for oc in range(8)]
                ov_pos = [0]

                ov_issued = {}

                def ov_issue(pos):
                    if pos >= len(ov_items) or pos in ov_issued:
                        return
                    kind, db, x, qcf = ov_items[pos]
                    if kind == "gu":
                        wi = x % 2
                        wg0_rel[wi].acquire(*Qs)
                        W(Qs, "qpc", 16)
                        gv = ctx.inc("qo0", sync.dma_start(wg0[wi], gate_b[0, db, x]))
                        wu0_rel[wi].acquire(*Qs)
                        W(Qs, "qpc", 32)
                        uv = ctx.inc("qo1", sync.dma_start(wu0[wi], up_b[0, db, x]))
                        ov_issued[pos] = (gv, uv)
                    else:
                        wi = x % 2
                        wd0_rel[wi].acquire(*Qs)
                        W(Qs, "qpc", 48)
                        ov_issued[pos] = ctx.inc("qo2", sync.dma_start(wd0[wi], down_b[0, db, x]))

                def ov_gu(pos, db, pc, qcf):
                    wi = pc % 2
                    gv, uv = ov_issued[pos]
                    brel[6].acquire(*Ts)
                    W(Ts, "qo0", gv)
                    W(Ts, "dve", ln2_done[qcf])
                    for c8 in range(8):
                        ins = t.matmul(bank(6), wg0[wi][:, c8 * 128:(c8 + 1) * 128],
                                       ln2[:, c8 * T + qcf * 512: c8 * T + (qcf + 1) * 512],
                                       start=(c8 == 0), stop=(c8 == 7))
                    mg = ctx.inc("pe", ins)
                    brel[7].acquire(*Ts)
                    W(Ts, "qo1", uv)
                    for c8 in range(8):
                        ins = t.matmul(bank(7), wu0[wi][:, c8 * 128:(c8 + 1) * 128],
                                       ln2[:, c8 * T + qcf * 512: c8 * T + (qcf + 1) * 512],
                                       start=(c8 == 0), stop=(c8 == 7))
                    mu = ctx.inc("pe", ins)
                    wg0_rel[wi].release("pe", mg)
                    wu0_rel[wi].release("pe", mu)
                    silg1_rel.acquire(*Ss)
                    W(Ss, "pe", mg)
                    sg = ctx.inc("act", s.activation(silg1, bank(6), AF.Exp, scale=-1.0))
                    W(Vs, "act", sg)
                    ctx.inc("dve", v.tensor_scalar_add(silg1, silg1, 1.0))
                    W(Vs, "dve", ctx.val("dve"))
                    ctx.inc("dve", v.reciprocal(silg1b, silg1))
                    W(Vs, "dve", ctx.val("dve"))
                    sv6 = ctx.inc("dve", v.tensor_tensor(silg1b, bank(6), silg1b, ALU.mult))
                    brel[6].release("dve", sv6)
                    silg1_rel.release("dve", sv6)
                    hq_rel[pc].acquire(*Vs)
                    W(Vs, "pe", mu)
                    hv = ctx.inc("dve", v.tensor_tensor(
                        hq[:, pc * 512:(pc + 1) * 512], bank(7), silg1b, ALU.mult))
                    brel[7].release("dve", hv)
                    hq_done[pc] = hv

                def ov_dn(pos, db, oc, qcf):
                    wi = oc % 2
                    dv_w = ov_issued[pos]
                    bk = 6 + oc % 2
                    brel[bk].acquire(*Ts)
                    W(Ts, "qo2", dv_w)
                    W(Ts, "dve", max(hq_done))
                    for pcx in range(8):
                        ins = t.matmul(bank(bk), wd0[wi][:, pcx * 128:(pcx + 1) * 128],
                                       hq[:, pcx * 512:(pcx + 1) * 512],
                                       start=(pcx == 0), stop=(pcx == 7))
                    md = ctx.inc("pe", ins)
                    wd0_rel[wi].release("pe", md)
                    W(Vs, "pe", md)
                    W(Vs, "act", p3_act[qcf])
                    W(Vs, "dve", max(xr_last0[oc][qcf], xr_dve[oc][qcf]))
                    dvx = ctx.inc("dve", v.tensor_tensor(
                        XR[:, oc * T + qcf * 512: oc * T + (qcf + 1) * 512],
                        XR[:, oc * T + qcf * 512: oc * T + (qcf + 1) * 512],
                        bank(bk), ALU.add))
                    xr_last0[oc][qcf] = dvx
                    brel[bk].release("dve", dvx)
                    if oc == 7:
                        for pc in range(8):
                            hq_rel[pc].release("pe", md)

                def ov_emit(n):
                    while n > 0 and ov_pos[0] < len(ov_items):
                        pos = ov_pos[0]
                        kind, db, x, qcf = ov_items[pos]
                        if qcf >= chunks_done[0]:
                            return
                        ov_issue(pos)
                        ov_issue(pos + 1)
                        if kind == "gu":
                            ov_gu(pos, db, x, qcf)
                        else:
                            ov_dn(pos, db, x, qcf)
                        ov_pos[0] += 1
                        n -= 1

                def p3_chunk(qc):
                    for h in range(2):
                        col0 = qc * 512 + h * 256
                        brel[7].acquire(*Ts)
                        for c8 in range(8):
                            ti = c8 % 2
                            xbt3_rel[ti].acquire(*Vs)
                            W(Vs, "dve", xr_dve[c8][qc])
                            cv = ctx.inc("dve", v.tensor_copy(
                                xbt3[ti], XR[:, c8 * T + col0: c8 * T + col0 + 256]))
                            W(Ts, "dve", cv)
                            mv = ctx.inc("pe", t.matmul(
                                bank(7)[0:1, 0:256], ones_bf[:, 0:1], xbt3[ti],
                                start=(c8 == 0), stop=(c8 == 7)))
                            xbt3_rel[ti].release("pe", mv)
                        for c8 in range(8):
                            ti = c8 % 2
                            xst3_rel[ti].acquire(*Ss)
                            W(Ss, "dve", xr_dve[c8][qc])
                            sv = ctx.inc("act", s.activation(
                                xst3[ti], XR[:, c8 * T + col0: c8 * T + col0 + 256],
                                AF.Square))
                            W(Ts, "act", sv)
                            mq = ctx.inc("pe", t.matmul(
                                bank(7)[0:1, 256:512], ones_bf[:, 0:1], xst3[ti],
                                start=(c8 == 0), stop=(c8 == 7)))
                            xst3_rel[ti].release("pe", mq)
                        W(Vs, "pe", mq)
                        dv1 = ctx.inc("dve", v.tensor_scalar_mul(
                            srow[0:1, 0:256], bank(7)[0:1, 0:256], 1.0 / D))
                        dv2 = ctx.inc("dve", v.tensor_scalar_mul(
                            srow[0:1, 512:768], bank(7)[0:1, 256:512], 1.0 / D))
                        W(Vs, "dve", dv2)
                        ctx.inc("dve", v.tensor_mul(srow[0:1, 1024:1280],
                                                    srow[0:1, 0:256], srow[0:1, 0:256]))
                        W(Vs, "dve", ctx.val("dve"))
                        dvv = ctx.inc("dve", v.tensor_sub(
                            srow[0:1, 1024:1280], srow[0:1, 512:768], srow[0:1, 1024:1280]))
                        W(Ss, "dve", dvv)
                        ctx.inc("act", s.activation(srow[0:1, 512:768], srow[0:1, 1024:1280],
                                                    AF.Ln, bias=eps_t[0:1, 0:1], scale=1.0))
                        W(Ss, "act", ctx.val("act"))
                        sv2 = ctx.inc("act", s.activation(srow[0:1, 512:768],
                                                          srow[0:1, 512:768],
                                                          AF.Exp, scale=-0.5))
                        W(Vs, "act", sv2)
                        dvn = ctx.inc("dve", v.scalar_tensor_tensor(
                            srow[0:1, 1024:1280], srow[0:1, 0:256], -1.0,
                            srow[0:1, 512:768], ALU.mult, ALU.mult))
                        W(Ts, "act", sv2)
                        W(Ts, "dve", dvn)
                        t.matmul(bank(7)[:, 0:256], ones_f, srow[0:1, 512:768],
                                 start=True, stop=True)
                        mvb = ctx.inc("pe", t.matmul(
                            bank(7)[:, 256:512], ones_f, srow[0:1, 1024:1280],
                            start=True, stop=True))
                        W(Vs, "pe", mvb)
                        lvh = 0
                        for c8 in range(8):
                            ni = c8 % 2
                            nrm3_rel[ni].acquire(*Vs)
                            dvt = ctx.inc("dve", v.tensor_tensor(
                                nrm3[ni], XR[:, c8 * T + col0: c8 * T + col0 + 256],
                                bank(7)[:, 0:256], ALU.mult))
                            W(Vs, "dve", dvt)
                            lvh = ctx.inc("dve", v.tensor_tensor(
                                ln2[:, c8 * T + col0: c8 * T + col0 + 256],
                                nrm3[ni], bank(7)[:, 256:512], ALU.add))
                            nrm3_rel[ni].release("dve", lvh)
                        brel[7].release("dve", lvh)
                        brel[7].acquire(*Ts)
                        W(Ts, "qsw", w_small)
                        W(Ts, "dve", lvh)
                        for c8 in range(8):
                            ins = t.matmul(bank(7)[0:7, 0:256],
                                           wr_b[:, c8 * 7:(c8 + 1) * 7],
                                           ln2[:, c8 * T + col0: c8 * T + col0 + 256],
                                           start=(c8 == 0), stop=(c8 == 7))
                        mlg = ctx.inc("pe", ins)
                        W(Ss, "pe", mlg)
                        slg = ctx.inc("act", s.activation(
                            lg[0:7, h * 256:(h + 1) * 256], bank(7)[0:7, 0:256], AF.Copy))
                        brel[7].release("act", slg)
                    ln2_done[qc] = lvh
                    brel[7].acquire(*Ts)
                    W(Ts, "act", slg)
                    W(Ts, "pool", ident_done)
                    for tt in range(4):
                        ins = t.transpose(PS[:, 7 * 512 + tt * 7: 7 * 512 + (tt + 1) * 7],
                                          lg[0:7, tt * 128:(tt + 1) * 128], ident[0:7, 0:7])
                    mtr = ctx.inc("pe", ins)
                    W(Vs, "pe", mtr)
                    dtr = ctx.inc("dve", v.tensor_copy(lgT, PS[:, 7 * 512: 7 * 512 + 28]))
                    brel[7].release("dve", dtr)

                    v3 = lambda a: a.rearrange("p (t j) -> p t j", j=7)
                    bc = lambda a: a.unsqueeze(2).broadcast_to([128, 4, 7])

                    def chain(ins):
                        r = ctx.inc("dve", ins)
                        W(Vs, "dve", r)
                        return r
                    W(Vs, "dve", dtr)
                    chain(v.reduce_max(st1, v3(lgT), axis=AX.X))
                    chain(v.tensor_tensor(v3(pr), v3(lgT), bc(st1), ALU.subtract))
                    W(Ss, "dve", ctx.val("dve"))
                    spr = ctx.inc("act", s.activation(pr, pr, AF.Exp))
                    W(Vs, "act", spr)
                    chain(v.reduce_sum(st2, v3(pr), axis=AX.X))
                    chain(v.reciprocal(st2, st2))
                    chain(v.tensor_tensor(v3(pr), v3(pr), bc(st2), ALU.mult))
                    chain(v.reduce_max(st1, v3(pr), axis=AX.X))
                    chain(v.tensor_tensor(v3(eq1), v3(pr), bc(st1), ALU.is_equal))
                    chain(v.tensor_scalar(eq2, eq1, -1.0, 1.0, ALU.mult, ALU.add))
                    chain(v.tensor_tensor(pr, pr, eq2, ALU.mult))
                    chain(v.reduce_max(st2, v3(pr), axis=AX.X))
                    chain(v.tensor_tensor(v3(eq2), v3(pr), bc(st2), ALU.is_equal))
                    chain(v.tensor_tensor(st3, st1, st2, ALU.add))
                    chain(v.tensor_scalar_add(st3, st3, 1e-9))
                    chain(v.reciprocal(st3, st3))
                    chain(v.tensor_tensor(st4, st1, st3, ALU.mult))
                    chain(v.tensor_tensor(st3, st2, st3, ALU.mult))
                    chain(v.tensor_tensor(v3(eq1), v3(eq1), bc(st4), ALU.mult))
                    chain(v.tensor_tensor(v3(eq2), v3(eq2), bc(st3), ALU.mult))
                    dcomb = ctx.inc("dve", v.tensor_tensor(combT, eq1, eq2, ALU.add))

                    brel[7].acquire(*Ts)
                    W(Ts, "dve", dcomb)
                    for tt in range(4):
                        ins = t.transpose(PS[0:7, 7 * 512 + tt * 128: 7 * 512 + (tt + 1) * 128],
                                          combT[:, tt * 7:(tt + 1) * 7], ident)
                    mtr2 = ctx.inc("pe", ins)
                    W(Vs, "pe", mtr2)
                    dcp = ctx.inc("dve", v.tensor_copy(
                        comb[0:7, qc * 512:(qc + 1) * 512], PS[0:7, 7 * 512:8 * 512]))
                    brel[7].release("dve", dcp)
                    comb_done[qc] = dcp
                    p3_act[qc] = ctx.val("act")

                for qc in range(QC):
                    for hp in range(NP):
                        qcol = slice(hp * T + qc * 512, hp * T + (qc + 1) * 512)
                        brel[4].acquire(*Ts)
                        brel[5].acquire(*Ts)
                        W(Ts, "dve", p1_dve)
                        W(Ts, "act", p1_act)
                        pv = 0
                        exp_of = {}

                        def issue_scores(kt):
                            sb = 2 * (kt % 2)
                            brel[sb].acquire(*Ts)
                            brel[sb + 1].acquire(*Ts)
                            t.matmul(bank(sb), K2[0:64, kt * 128:(kt + 1) * 128],
                                     Q[0:64, qcol], start=True, stop=True,
                                     tile_position=(0, 0))
                            ins = t.matmul(bank(sb + 1), K2[64:128, kt * 128:(kt + 1) * 128],
                                           Q[64:128, qcol], start=True, stop=True,
                                           tile_position=(64, 0))
                            mv = ctx.inc("pe", ins)
                            pt = kt % NPT
                            PT_rel[pt].acquire(*Ss)
                            W(Ss, "pe", mv)
                            W(Ss, "dve", bias_done)
                            ev = ctx.inc("act", s.activation(
                                PT[pt], PS[:, sb * 512: sb * 512 + 1024], AF.Exp,
                                bias=bias_hd[:, hp:hp + 1], scale=0.125))
                            brel[sb].release("act", ev)
                            brel[sb + 1].release("act", ev)
                            exp_of[kt] = ev

                        issue_scores(0)
                        issue_scores(1)
                        for kt in range(KT):
                            if kt + 2 < KT:
                                issue_scores(kt + 2)
                            pt = kt % NPT
                            W(Ts, "act", exp_of[kt])
                            t.matmul(bank(4), VA[:, kt * 65:kt * 65 + 128],
                                     PT[pt][:, 0:512], start=(kt == 0), stop=(kt == KT - 1))
                            pv = ctx.inc("pe", t.matmul(
                                bank(5), VA[:, kt * 65:kt * 65 + 128],
                                PT[pt][:, 512:1024], start=(kt == 0), stop=(kt == KT - 1)))
                            PT_rel[pt].release("pe", pv)
                            if kt % 8 == 7 and kt < KT - 1:
                                ov_emit(1)
                        rd_rel.acquire(*Ss)
                        W(Ss, "pe", pv)
                        s.activation(rdrow2[0:1, 0:512], PS[64:65, 4 * 512:5 * 512], AF.Ln)
                        lnv = ctx.inc("act", s.activation(
                            rdrow2[0:1, 512:1024], PS[64:65, 5 * 512:6 * 512], AF.Ln))
                        W(Ss, "act", lnv)
                        rde = ctx.inc("act", s.activation(rdrow2, rdrow2, AF.Exp, scale=-1.0))
                        brel[6].acquire(*Ts)
                        W(Ts, "act", rde)
                        t.matmul(PS[0:64, 6 * 512:7 * 512], ones_f[0:1, 0:64],
                                 rdrow2[0:1, 0:512], start=True, stop=True)
                        mb = ctx.inc("pe", t.matmul(
                            PS[64:128, 6 * 512:7 * 512], ones_f[0:1, 0:64],
                            rdrow2[0:1, 512:1024], start=True, stop=True))
                        rd_rel.release("pe", mb)
                        a2 = 0
                        for hh in range(2):
                            rows = slice(64 * hh, 64 * hh + 64)
                            ab = 4 + hh
                            W(Ss, "dve", a2_prev[0])
                            sc = ctx.inc("act", s.activation(
                                att_s[rows, :], PS[0:64, ab * 512:(ab + 1) * 512], AF.Copy))
                            brel[ab].release("act", sc)
                            W(Vs, "act", sc)
                            W(Vs, "pe", mb)
                            m1 = ctx.inc("dve", v.tensor_tensor(
                                att_s[rows, :], att_s[rows, :],
                                PS[rows, 6 * 512:7 * 512], ALU.mult))
                            W(Vs, "qxr", xr_load)
                            a2 = ctx.inc("dve", v.tensor_tensor(
                                XR[rows, hp * T + qc * 512: hp * T + (qc + 1) * 512],
                                XR[rows, hp * T + qc * 512: hp * T + (qc + 1) * 512],
                                att_s[rows, :], ALU.add))
                        brel[6].release("dve", a2)
                        a2_prev[0] = a2
                        xr_dve[hp][qc] = a2
                        ov_emit(1)
                    p3_chunk(qc)
                    chunks_done[0] = qc + 1
                    ov_emit(2)

                ov_emit(len(ov_items))
                assert ov_pos[0] == len(ov_items)
                tap("X2", [128, 8 * T], XR, "dve")
                tap("ln2", [128, 4 * T], ln2.bitcast(F32), "dve")
                tap("comb", [7, T], comb, "dve")
                ctx.barrier(ALL_ENG, exclude=("qpc",))
            ctx.barrier(ALL_ENG, exclude=("qpc",))

        with ExitStack() as fb:
            fb.enter_context(nc.named_scope("P4"))
            SB4 = lambda n, sh, dt=F32: fb.enter_context(nc.sbuf_tensor(n, sh, dt)).ap()
            silg = SB4("silg", [128, T])
            silg_rel = Rel(ctx)
            hsb = SB4("hsb", [128, 8 * T], BF16)
            hsb_rel = [Rel(ctx) for _ in range(8)]
            cbs = SB4("cbs", [128, T])
            cbs_rel = Rel(ctx)
            crow = SB4("crow", [1, T])
            crow_rel = Rel(ctx)
            wgb = [SB4(f"wgb{i}", [128, 8 * 128], BF16) for i in range(2)]
            wgb_rel = [Rel(ctx) for _ in range(2)]
            wub = [SB4(f"wub{i}", [128, 8 * 128], BF16) for i in range(2)]
            wub_rel = [Rel(ctx) for _ in range(2)]
            wdb = [SB4(f"wdb{i}", [128, 8 * 128], BF16) for i in range(2)]
            wdb_rel = [Rel(ctx) for _ in range(2)]

            dcp4 = comb_done[QC - 1]
            xr_last = [[max(xr_dve[oc][qc], xr_last0[oc][qc]) for qc in range(QC)]
                       for oc in range(8)]
            ln2_all = max(ln2_done)
            for e in range(1, 8):
                if e > 0:
                    crow_rel.acquire(*Qs)
                    W(Qs, "dve", dcp4)
                    cr = ctx.inc("qg", sync.dma_start(crow, comb[e - 1:e, :]))
                    cbs_rel.acquire(*Vs)
                    for qc in range(QC):
                        bk = 2 + qc % 2
                        brel[bk].acquire(*Ts)
                        W(Ts, "qg", cr)
                        ins = t.matmul(bank(bk), ones_f,
                                       crow[0:1, qc * 512:(qc + 1) * 512],
                                       start=True, stop=True)
                        mv = ctx.inc("pe", ins)
                        W(Vs, "pe", mv)
                        dv = ctx.inc("dve", v.tensor_copy(
                            cbs[:, qc * 512:(qc + 1) * 512], bank(bk)))
                        brel[bk].release("dve", dv)
                    crow_rel.release("pe", ctx.val("pe"))
                    cb_done = ctx.val("dve")
                else:
                    cb_done = 0

                for db in range(4):
                    h_done = 0
                    for pc in range(8):
                        wi = pc % 2
                        wgb_rel[wi].acquire(*Qs)
                        W(Qs, "qpc", qpc_all)
                        gv = ctx.inc(f"qw{wi}", sync.dma_start(wgb[wi], gate_b[e, db, pc]))
                        wub_rel[wi].acquire(*Qs)
                        uv = ctx.inc(f"qw{2 + wi}", sync.dma_start(wub[wi], up_b[e, db, pc]))
                        for qc in range(QC):
                            brel[qc].acquire(*Ts)
                        W(Ts, f"qw{wi}", gv)
                        W(Ts, "dve", ln2_all)
                        ins = None
                        for qc in range(QC):
                            for c8 in range(8):
                                ins = t.matmul(
                                    bank(qc), wgb[wi][:, c8 * 128:(c8 + 1) * 128],
                                    ln2[:, c8 * T + qc * 512: c8 * T + (qc + 1) * 512],
                                    start=(c8 == 0), stop=(c8 == 7))
                        mg = ctx.inc("pe", ins)
                        for qc in range(QC):
                            brel[4 + qc].acquire(*Ts)
                        W(Ts, f"qw{2 + wi}", uv)
                        for qc in range(QC):
                            for c8 in range(8):
                                ins = t.matmul(
                                    bank(4 + qc), wub[wi][:, c8 * 128:(c8 + 1) * 128],
                                    ln2[:, c8 * T + qc * 512: c8 * T + (qc + 1) * 512],
                                    start=(c8 == 0), stop=(c8 == 7))
                        mu = ctx.inc("pe", ins)
                        wgb_rel[wi].release("pe", mg)
                        wub_rel[wi].release("pe", mu)
                        silg_rel.acquire(*Ss)
                        W(Ss, "pe", mg)
                        ssg = ctx.inc("act", s.activation(silg, PS[:, 0:2048], AF.Silu))
                        for qc in range(QC):
                            brel[qc].release("act", ssg)
                        hsb_rel[pc].acquire(*Vs)
                        W(Vs, "act", ssg)
                        W(Vs, "pe", mu)
                        if e > 0:
                            W(Vs, "dve", cb_done)
                            ctx.inc("dve", v.tensor_tensor(silg, PS[:, 2048:4096],
                                                           silg, ALU.mult))
                            W(Vs, "dve", ctx.val("dve"))
                            hv = ctx.inc("dve", v.tensor_tensor(
                                hsb[:, pc * T:(pc + 1) * T], silg, cbs, ALU.mult))
                        else:
                            hv = ctx.inc("dve", v.tensor_tensor(
                                hsb[:, pc * T:(pc + 1) * T], PS[:, 2048:4096],
                                silg, ALU.mult))
                        silg_rel.release("dve", hv)
                        for qc in range(QC):
                            brel[4 + qc].release("dve", hv)
                        h_done = hv
                    for oc in range(8):
                        wi = oc % 2
                        wdb_rel[wi].acquire(*Qs)
                        dv_w = ctx.inc(f"qw{4 + wi}", sync.dma_start(wdb[wi], down_b[e, db, oc]))
                        for qc in range(QC):
                            bk = qc
                            brel[bk].acquire(*Ts)
                            W(Ts, f"qw{4 + wi}", dv_w)
                            W(Ts, "dve", h_done)
                            ins = None
                            for pcx in range(8):
                                ins = t.matmul(
                                    bank(bk), wdb[wi][:, pcx * 128:(pcx + 1) * 128],
                                    hsb[:, pcx * T + qc * 512: pcx * T + (qc + 1) * 512],
                                    start=(pcx == 0), stop=(pcx == 7))
                            md = ctx.inc("pe", ins)
                            W(Vs, "pe", md)
                            W(Vs, "dve", xr_last[oc][qc])
                            dvx = ctx.inc("dve", v.tensor_tensor(
                                XR[:, oc * T + qc * 512: oc * T + (qc + 1) * 512],
                                XR[:, oc * T + qc * 512: oc * T + (qc + 1) * 512],
                                bank(bk), ALU.add))
                            xr_last[oc][qc] = dvx
                            brel[bk].release("dve", dvx)
                            if e == 7 and db == 3:
                                W(Qs, "dve", dvx)
                                ctx.inc("st", sync.dma_start(
                                    out[oc * 128:(oc + 1) * 128, qc * 512:(qc + 1) * 512],
                                    XR[:, oc * T + qc * 512: oc * T + (qc + 1) * 512]))
                        wdb_rel[wi].release("pe", ctx.val("pe"))
                    pe_after_down = ctx.val("pe")
                    for pc in range(8):
                        hsb_rel[pc].release("pe", pe_after_down)
                cbs_rel.release("dve", ctx.val("dve"))

            sync.wait_ge(ctx.sems["st"][0], ctx.val("st"))
    return nc, tap_aps



def host_inputs(inputs, core):
    b, half = core // 2, core % 2
    x = np.asarray(inputs["L0_attn_x"])
    xT = np.ascontiguousarray(x[b].T)
    x_own = np.ascontiguousarray(xT[:, half * T:(half + 1) * T])
    x_oth = np.ascontiguousarray(xT[:, (1 - half) * T:(2 - half) * T])
    x_perm = np.ascontiguousarray(np.concatenate([x_oth, x_own], axis=1))

    def tile_in_by_out(w):
        wt = np.ascontiguousarray(np.asarray(w).T)
        o = w.shape[0]
        return np.ascontiguousarray(
            wt.reshape(8, 128, o).transpose(1, 0, 2).reshape(128, 8 * o))

    wq_h = tile_in_by_out(inputs["L0_attn_Wq"])
    wk_h = tile_in_by_out(inputs["L0_attn_Wk"])
    wv_h = tile_in_by_out(inputs["L0_attn_Wv"])
    wr_h = tile_in_by_out(inputs["L0_ffn_router_w"])

    def tile_gateup(w):
        wt = np.asarray(w).T
        r = wt.reshape(8, 128, 4, 8, 128)
        return np.ascontiguousarray(r.transpose(2, 3, 1, 0, 4).reshape(4, 8, 128, 1024))

    def tile_down(w):
        wt = np.asarray(w).T
        r = wt.reshape(4, 8, 128, 8, 128)
        return np.ascontiguousarray(r.transpose(0, 3, 2, 1, 4).reshape(4, 8, 128, 1024))

    gate_h = np.stack([tile_gateup(inputs["L0_ffn_shared_gate"])]
                      + [tile_gateup(np.asarray(inputs["L0_ffn_routed_gate"])[i])
                         for i in range(7)])
    up_h = np.stack([tile_gateup(inputs["L0_ffn_shared_up"])]
                    + [tile_gateup(np.asarray(inputs["L0_ffn_routed_up"])[i])
                       for i in range(7)])
    down_h = np.stack([tile_down(inputs["L0_ffn_shared_down"])]
                      + [tile_down(np.asarray(inputs["L0_ffn_routed_down"])[i])
                         for i in range(7)])

    return dict(x_full=x_perm, x_own=x_own, wq=wq_h, wk=wk_h, wv=wv_h, wr=wr_h,
                gate_t=gate_h, up_t=up_h, down_t=down_h)


def kernel(**inputs):
    nc, _ = build()
    shared = None
    in_maps = []
    for c in range(8):
        m = host_inputs(inputs, c)
        if shared is None:
            shared = {k: m[k] for k in ("wq", "wk", "wv", "wr", "gate_t", "up_t", "down_t")}
        else:
            for k in shared:
                m[k] = shared[k]
        in_maps.append(m)
    res = run_bass_kernel_spmd(nc, in_maps, core_ids=list(range(8)))
    outT = np.zeros((4, D, S), dtype=np.float32)
    for c in range(8):
        b, half = c // 2, c % 2
        outT[b][:, half * T:(half + 1) * T] = res.results[c]["out"]
    return np.ascontiguousarray(outT.transpose(0, 2, 1))


# version 10
# speedup vs baseline: 1.1333x; 1.1333x over previous
import numpy as np
from contextlib import ExitStack

import concourse.bass as bass
import concourse.mybir as mybir
from concourse.bass_utils import run_bass_kernel_spmd

F32 = mybir.dt.float32
BF16 = mybir.dt.bfloat16
AF = mybir.ActivationFunctionType
ALU = mybir.AluOpType
AX = mybir.AxisListType

D = 1024
S = 4096
T = 2048
NP = 8
EPS = 1e-5
KT = S // 128
QC = T // 512
CW = 256
NPT = 3


class Ctx:
    def __init__(self, nc, stack):
        self.nc = nc
        self.stack = stack
        self.sems = {}
        self.hwm = {}

    def sem(self, name, step=1):
        h = self.stack.enter_context(self.nc.semaphore(name))
        self.sems[name] = [h, step, 0]

    def inc(self, name, inst):
        e = self.sems[name]
        inst.then_inc(e[0], e[1])
        e[2] += e[1]
        return e[2]

    def val(self, name):
        return self.sems[name][2]

    def wait(self, ekey, eng, name, v):
        if v <= 0:
            return
        key = (ekey, name)
        if self.hwm.get(key, 0) >= v:
            return
        eng.wait_ge(self.sems[name][0], v)
        self.hwm[key] = v

    def barrier(self, engines, exclude=()):
        for ekey, eng in engines:
            for name in self.sems:
                if name in exclude:
                    continue
                self.wait(ekey, eng, name, self.val(name))


class Rel:
    def __init__(self, ctx):
        self.ctx = ctx
        self.rel = []

    def acquire(self, ekey, eng):
        for (sn, vv) in self.rel:
            self.ctx.wait(ekey, eng, sn, vv)
        self.rel = []

    def release(self, sem_name, val):
        self.rel.append((sem_name, val))


def build(taps=()):
    nc = bass.Bass("TRN2", target_bir_lowering=False)
    dr = lambda n, sh, kind: nc.dram_tensor(n, sh, F32, kind=kind).ap()

    x_full = dr("x_full", [D, S], "ExternalInput")
    x_own = dr("x_own", [D, T], "ExternalInput")
    wq = dr("wq", [128, 8 * 1024], "ExternalInput")
    wk = dr("wk", [128, 8 * 64], "ExternalInput")
    wv = dr("wv", [128, 8 * 64], "ExternalInput")
    wr = dr("wr", [128, 8 * 7], "ExternalInput")
    gate_t = dr("gate_t", [8, 4, 8, 128, 1024], "ExternalInput")
    up_t = dr("up_t", [8, 4, 8, 128, 1024], "ExternalInput")
    down_t = dr("down_t", [8, 4, 8, 128, 1024], "ExternalInput")
    out = dr("out", [D, T], "ExternalOutput")

    tap_aps = {}

    with ExitStack() as stack:
        ctx = Ctx(nc, stack)
        SB = lambda n, sh, dt=F32: stack.enter_context(nc.sbuf_tensor(n, sh, dt)).ap()
        sync, v, s, t, g = nc.sync, nc.vector, nc.scalar, nc.tensor, nc.gpsimd
        Ts, Ss, Vs, Gs, Qs = ("t", t), ("s", s), ("v", v), ("g", g), ("q", sync)
        ALL_ENG = [Ts, Ss, Vs, Gs, Qs]

        for name, step in ([("qg", 16), ("qxr", 16), ("qwq", 16), ("qsw", 16), ("qpc", 16),
                            ("pe", 1), ("act", 1),
                            ("dve", 1), ("pool", 1), ("st", 16)]
                           + [(f"qw{i}", 16) for i in range(6)]
                           + [(f"qo{i}", 16) for i in range(3)]
                           + [(f"qxf{i}", 16) for i in range(2)]):
            ctx.sem(name, step)
        W = lambda ee, name, val: ctx.wait(ee[0], ee[1], name, val)

        def tap(name, shape, src_ap, dep_sem):
            if name not in taps:
                return
            ap = dr("tap_" + name, shape, "ExternalOutput")
            tap_aps[name] = ap
            W(Qs, dep_sem, ctx.val(dep_sem))
            ctx.inc("st", sync.dma_start(ap, src_ap))

        XR = SB("XR", [128, 8 * T])
        ln2 = SB("ln2", [128, 8 * T], BF16)
        comb = SB("comb", [7, T])
        lg = SB("lg", [7, 512])
        lgT = SB("lgT", [128, 4 * 7])
        st1 = SB("st1", [128, 4]); st2 = SB("st2", [128, 4])
        st3 = SB("st3", [128, 4]); st4 = SB("st4", [128, 4])
        pr = SB("pr", [128, 28]); eq1 = SB("eq1", [128, 28])
        eq2 = SB("eq2", [128, 28]); combT = SB("combT", [128, 28])
        PS = stack.enter_context(nc.psum_tensor("PS", [128, 4096], F32)).ap()
        bank = lambda i: PS[:, i * 512:(i + 1) * 512]
        brel = [Rel(ctx) for _ in range(8)]

        wk_b = SB("wk_b", [128, 8 * 64], BF16)
        wv_b = SB("wv_b", [128, 8 * 64], BF16)
        wr_b = SB("wr_b", [128, 8 * 7], BF16)
        ones_bf = SB("ones_bf", [128, 2], BF16)
        ones_f = SB("ones_f", [1, 128])
        eps_t = SB("eps_t", [1, 2])
        ident = SB("ident", [128, 128])
        srow = SB("srow", [1, 3 * 512])
        mrow = SB("mrow", [1, 16])
        arow = SB("arow", [1, 16])

        ctx.inc("dve", v.memset(ones_bf, 1.0))
        ctx.inc("dve", v.memset(ones_f, 1.0))
        consts_dve = ctx.inc("dve", v.memset(eps_t, EPS))
        ctx.inc("pool", g.memset(ident, 1.0))
        W(Gs, "pool", ctx.val("pool"))
        ident_done = ctx.inc("pool", g.affine_select(
            ident, ident, [[1, 128]], ALU.is_equal, 0.0, base=0, channel_multiplier=-1))

        ctx.inc("qsw", g.dma_start(wk_b, wk))
        ctx.inc("qsw", g.dma_start(wv_b, wv))
        w_small = ctx.inc("qsw", g.dma_start(wr_b, wr))
        wq_done = None

        xr_load = ctx.inc("qxr", sync.dma_start(
            XR.rearrange("p (c n) -> p c n", c=8),
            x_own.rearrange("(c p) n -> p c n", p=128)))

        with ExitStack() as fa:
            SBA = lambda n, sh, dt=F32: fa.enter_context(nc.sbuf_tensor(n, sh, dt)).ap()
            Q = SBA("Q", [128, 8 * T], BF16)
            K2 = SBA("K2", [128, S], BF16)
            VA = SBA("VA", [128, KT * 65 + 64], BF16)
            va_ones = ctx.inc("dve", v.memset(VA, 1.0))

            with ExitStack() as fa1:
                fa1.enter_context(nc.named_scope("P1"))
                SB1 = lambda n, sh, dt=F32: fa1.enter_context(nc.sbuf_tensor(n, sh, dt)).ap()
                wq_b = SB1("wq_b", [128, 8 * 1024], BF16)
                xf = [SB1(f"xf{i}", [128, 8 * CW]) for i in range(2)]
                xf_rel = [Rel(ctx) for _ in range(2)]
                xbt = [SB1(f"xbt{i}", [128, CW], BF16) for i in range(2)]
                xbt_rel = [Rel(ctx) for _ in range(2)]
                xst = [SB1(f"xst{i}", [128, CW], BF16) for i in range(2)]
                xst_rel = [Rel(ctx) for _ in range(2)]
                lnb = [SB1(f"lnb{i}", [128, 8 * CW], BF16) for i in range(2)]
                lnb_rel = [Rel(ctx) for _ in range(2)]
                nrm = [SB1(f"nrm{i}", [128, CW]) for i in range(2)]
                nrm_rel = [Rel(ctx) for _ in range(2)]

                wq_done = ctx.inc("qwq", g.dma_start(wq_b, wq))

                xr_src = x_full.rearrange("(c p) n -> p c n", p=128)
                for j in range(16):
                    bi = j % 2
                    own = j >= 8
                    xf_rel[bi].acquire(*Qs)
                    dv = ctx.inc(f"qxf{bi}", sync.dma_start(
                        xf[bi].rearrange("p (c n) -> p c n", c=8),
                        xr_src[:, :, j * CW:(j + 1) * CW]))
                    brel[0].acquire(*Ts)
                    W(Ts, "dve", consts_dve)
                    for c8 in range(8):
                        ti = c8 % 2
                        xbt_rel[ti].acquire(*Vs)
                        W(Vs, f"qxf{bi}", dv)
                        cv = ctx.inc("dve", v.tensor_copy(
                            xbt[ti], xf[bi][:, c8 * CW:(c8 + 1) * CW]))
                        W(Ts, "dve", cv)
                        mv = ctx.inc("pe", t.matmul(
                            bank(0)[0:1, 0:CW], ones_bf[:, 0:1], xbt[ti],
                            start=(c8 == 0), stop=(c8 == 7)))
                        xbt_rel[ti].release("pe", mv)
                    mv_sum = mv
                    brel[1].acquire(*Ts)
                    for c8 in range(8):
                        ti = c8 % 2
                        xst_rel[ti].acquire(*Ss)
                        W(Ss, f"qxf{bi}", dv)
                        sv = ctx.inc("act", s.activation(
                            xst[ti], xf[bi][:, c8 * CW:(c8 + 1) * CW], AF.Square))
                        W(Ts, "act", sv)
                        mq = ctx.inc("pe", t.matmul(
                            bank(1)[0:1, 0:CW], ones_bf[:, 0:1], xst[ti],
                            start=(c8 == 0), stop=(c8 == 7)))
                        xst_rel[ti].release("pe", mq)
                    mv_sq = mq
                    W(Vs, "pe", mv_sq)
                    dv1 = ctx.inc("dve", v.tensor_scalar_mul(
                        srow[0:1, 0:CW], bank(0)[0:1, 0:CW], 1.0 / D))
                    brel[0].release("dve", dv1)
                    dv2 = ctx.inc("dve", v.tensor_scalar_mul(
                        srow[0:1, 512:512 + CW], bank(1)[0:1, 0:CW], 1.0 / D))
                    W(Vs, "dve", dv2)
                    ctx.inc("dve", v.tensor_mul(srow[0:1, 1024:1024 + CW],
                                                srow[0:1, 0:CW], srow[0:1, 0:CW]))
                    W(Vs, "dve", ctx.val("dve"))
                    dvv = ctx.inc("dve", v.tensor_sub(
                        srow[0:1, 1024:1024 + CW], srow[0:1, 512:512 + CW],
                        srow[0:1, 1024:1024 + CW]))
                    W(Ss, "dve", dvv)
                    W(Ss, "dve", consts_dve)
                    ctx.inc("act", s.activation(srow[0:1, 512:512 + CW],
                                                srow[0:1, 1024:1024 + CW],
                                                AF.Ln, bias=eps_t[0:1, 0:1], scale=1.0))
                    W(Ss, "act", ctx.val("act"))
                    sv2 = ctx.inc("act", s.activation(srow[0:1, 512:512 + CW],
                                                      srow[0:1, 512:512 + CW],
                                                      AF.Exp, scale=-0.5))
                    W(Vs, "act", sv2)
                    dvn = ctx.inc("dve", v.scalar_tensor_tensor(
                        srow[0:1, 1024:1024 + CW], srow[0:1, 0:CW], -1.0,
                        srow[0:1, 512:512 + CW], ALU.mult, ALU.mult))
                    brel[2].acquire(*Ts)
                    W(Ts, "act", sv2)
                    W(Ts, "dve", dvn)
                    t.matmul(bank(1)[:, 0:CW], ones_f, srow[0:1, 512:512 + CW],
                             start=True, stop=True)
                    mvb = ctx.inc("pe", t.matmul(
                        bank(2)[:, 0:CW], ones_f, srow[0:1, 1024:1024 + CW],
                        start=True, stop=True))
                    lnb_rel[bi].acquire(*Vs)
                    W(Vs, "pe", mvb)
                    lv = 0
                    for c8 in range(8):
                        ni = c8 % 2
                        nrm_rel[ni].acquire(*Vs)
                        dvt = ctx.inc("dve", v.tensor_tensor(
                            nrm[ni], xf[bi][:, c8 * CW:(c8 + 1) * CW],
                            bank(1)[:, 0:CW], ALU.mult))
                        W(Vs, "dve", dvt)
                        lv = ctx.inc("dve", v.tensor_tensor(
                            lnb[bi][:, c8 * CW:(c8 + 1) * CW], nrm[ni],
                            bank(2)[:, 0:CW], ALU.add))
                        nrm_rel[ni].release("dve", lv)
                    xf_rel[bi].release("dve", lv)
                    brel[1].release("dve", lv)
                    brel[2].release("dve", lv)

                    brel[3].acquire(*Ts)
                    W(Ts, "qsw", w_small)
                    W(Ts, "dve", lv)
                    for c8 in range(8):
                        ins = t.matmul(bank(3)[0:64, 0:CW], wk_b[:, c8 * 64:(c8 + 1) * 64],
                                       lnb[bi][:, c8 * CW:(c8 + 1) * CW],
                                       start=(c8 == 0), stop=(c8 == 7))
                    mk = ctx.inc("pe", ins)
                    W(Ss, "pe", mk)
                    s.activation(K2[0:64, j * CW:(j + 1) * CW], bank(3)[0:64, 0:CW], AF.Copy)
                    se = ctx.inc("act", s.activation(K2[64:128, j * CW:(j + 1) * CW],
                                                     bank(3)[0:64, 0:CW], AF.Copy))
                    brel[3].release("act", se)
                    brel[4].acquire(*Ts)
                    for kt4 in range(CW // 128):
                        for c8 in range(8):
                            ins = t.matmul(
                                PS[:, 4 * 512 + kt4 * 64: 4 * 512 + (kt4 + 1) * 64],
                                lnb[bi][:, c8 * CW + kt4 * 128: c8 * CW + (kt4 + 1) * 128],
                                wv_b[:, c8 * 64:(c8 + 1) * 64],
                                start=(c8 == 0), stop=(c8 == 7))
                    mvv = ctx.inc("pe", ins)
                    W(Vs, "pe", mvv)
                    W(Vs, "dve", va_ones)
                    kt0 = j * (CW // 128)
                    dvv2 = ctx.inc("dve", v.tensor_copy(
                        VA[:, kt0 * 65:(kt0 + CW // 128) * 65]
                        .rearrange("p (k n) -> p k n", k=CW // 128)[:, :, 0:64],
                        PS[:, 4 * 512: 4 * 512 + (CW // 128) * 64]
                        .rearrange("p (k n) -> p k n", k=CW // 128)))
                    brel[4].release("dve", dvv2)
                    qlast = mvv
                    if own:
                        jj = j - 8
                        for hp in range(8):
                            bk = 5 + (hp % 2)
                            brel[bk].acquire(*Ts)
                            W(Ts, "qwq", wq_done)
                            for c8 in range(8):
                                ins = t.matmul(
                                    bank(bk)[:, 0:CW],
                                    wq_b[:, c8 * 1024 + hp * 128: c8 * 1024 + (hp + 1) * 128],
                                    lnb[bi][:, c8 * CW:(c8 + 1) * CW],
                                    start=(c8 == 0), stop=(c8 == 7))
                            mq2 = ctx.inc("pe", ins)
                            W(Ss, "pe", mq2)
                            qe = ctx.inc("act", s.activation(
                                Q[:, hp * T + jj * CW: hp * T + (jj + 1) * CW],
                                bank(bk)[:, 0:CW], AF.Copy))
                            brel[bk].release("act", qe)
                        qlast = ctx.val("pe")
                    lnb_rel[bi].release("pe", qlast)

                ctx.barrier(ALL_ENG, exclude=("qpc",))

            p1_act = ctx.val("act")
            p1_dve = ctx.val("dve")
            tap("K2", [128, S // 2], K2.bitcast(F32), "act")
            tap("VA", [128, (KT * 65 + 1) // 2], VA.bitcast(F32), "dve")
            tap("Q", [128, 4 * T], Q.bitcast(F32), "act")

            with ExitStack() as fa2:
                fa2.enter_context(nc.named_scope("P2"))
                SB2 = lambda n, sh, dt=F32: fa2.enter_context(nc.sbuf_tensor(n, sh, dt)).ap()
                PT = [SB2(f"PT{i}", [128, 1024], BF16) for i in range(NPT)]
                PT_rel = [Rel(ctx) for _ in range(NPT)]
                att_s = SB2("att_s", [128, 512])
                bias_hd = SB2("bias_hd", [128, 8])
                rdrow2 = SB2("rdrow2", [1, 1024])
                rd_rel = Rel(ctx)
                xbt3 = [SB2("xbt3_0", [128, 256], BF16)]
                xbt3_rel = [Rel(ctx)]
                xst3 = [SB2("xst3_0", [128, 256], BF16)]
                xst3_rel = [Rel(ctx)]
                nrm3 = [SB2("nrm3_0", [128, 256])]
                nrm3_rel = [Rel(ctx)]

                with ExitStack() as fp:
                    SBP = lambda n, sh, dt=F32: fp.enter_context(nc.sbuf_tensor(n, sh, dt)).ap()
                    ksq = SBP("ksq", [64, S], BF16)
                    qsq = [SBP(f"qsq{i}", [128, T], BF16) for i in range(2)]
                    qsq_rel = [Rel(ctx) for _ in range(2)]

                    W(Ss, "act", p1_act)
                    kq = ctx.inc("act", s.activation(ksq, K2[0:64, :], AF.Square))
                    for c in range(8):
                        brel[7].acquire(*Ts)
                        W(Ts, "act", kq)
                        ins = t.matmul(bank(7)[0:1, :], ones_bf[0:64, 0:1],
                                       ksq[:, c * 512:(c + 1) * 512], start=True, stop=True)
                        mv = ctx.inc("pe", ins)
                        W(Vs, "pe", mv)
                        dv = ctx.inc("dve", v.reduce_max(mrow[0:1, c:c + 1],
                                                         bank(7)[0:1, :], axis=AX.X))
                        brel[7].release("dve", dv)
                    W(Vs, "dve", ctx.val("dve"))
                    ctx.inc("dve", v.reduce_max(mrow[0:1, 8:9], mrow[0:1, 0:8], axis=AX.X))

                    for hp in range(NP):
                        qi = hp % 2
                        qsq_rel[qi].acquire(*Ss)
                        W(Ss, "act", p1_act)
                        sv = ctx.inc("act", s.activation(qsq[qi], Q[:, hp * T:(hp + 1) * T],
                                                         AF.Square))
                        for qc in range(QC):
                            brel[7].acquire(*Ts)
                            W(Ts, "act", sv)
                            ins = t.matmul(bank(7)[0:1, :], ones_bf[:, 0:1],
                                           qsq[qi][:, qc * 512:(qc + 1) * 512],
                                           start=True, stop=True)
                            mv = ctx.inc("pe", ins)
                            W(Vs, "pe", mv)
                            dv = ctx.inc("dve", v.reduce_max(
                                mrow[0:1, 12 + qc:13 + qc], bank(7)[0:1, :], axis=AX.X))
                            brel[7].release("dve", dv)
                        qsq_rel[qi].release("pe", ctx.val("pe"))
                        W(Vs, "dve", ctx.val("dve"))
                        ctx.inc("dve", v.reduce_max(mrow[0:1, 10:11], mrow[0:1, 12:16],
                                                    axis=AX.X))
                        W(Vs, "dve", ctx.val("dve"))
                        ctx.inc("dve", v.tensor_add(arow[0:1, 0:1], mrow[0:1, 10:11],
                                                    mrow[0:1, 8:9]))
                        W(Vs, "dve", ctx.val("dve"))
                        dvb2 = ctx.inc("dve", v.tensor_scalar_mul(
                            arow[0:1, 0:1], arow[0:1, 0:1], -1.0 / 16.0))
                        brel[6].acquire(*Ts)
                        W(Ts, "dve", dvb2)
                        ins = t.matmul(PS[:, 6 * 512:6 * 512 + 1], ones_f,
                                       arow[0:1, 0:1], start=True, stop=True)
                        mv = ctx.inc("pe", ins)
                        W(Vs, "pe", mv)
                        dvc = ctx.inc("dve", v.tensor_copy(bias_hd[:, hp:hp + 1],
                                                           PS[:, 6 * 512:6 * 512 + 1]))
                        brel[6].release("dve", dvc)
                    bias_done = ctx.val("dve")
                    ctx.barrier(ALL_ENG, exclude=("qpc",))

                wf0 = [SB2(f"wf0_{i}", [128, 1024]) for i in range(2)]
                wf0_rel = [Rel(ctx) for _ in range(2)]
                wg0 = [SB2(f"wg0_{i}", [128, 8 * 128], BF16) for i in range(2)]
                wg0_rel = [Rel(ctx) for _ in range(2)]
                wu0 = [SB2(f"wu0_{i}", [128, 8 * 128], BF16) for i in range(2)]
                wu0_rel = [Rel(ctx) for _ in range(2)]
                wd0 = [SB2(f"wd0_{i}", [128, 8 * 128], BF16) for i in range(2)]
                wd0_rel = [Rel(ctx) for _ in range(2)]
                hq = SB2("hq", [128, 8 * 512], BF16)
                hq_rel = [Rel(ctx) for _ in range(8)]
                silg1 = SB2("silg1", [128, 512])
                silg1b = SB2("silg1b", [128, 512])
                silg1_rel = Rel(ctx)

                xr_dve = [[0] * QC for _ in range(NP)]
                ln2_done = [0] * QC
                comb_done = [0] * QC
                p3_act = [0] * QC
                a2_prev = [0]

                xr_last0 = [[0] * QC for _ in range(8)]
                hq_done = [0] * 8
                chunks_done = [0]
                ov_items = []
                for qcf in range(QC):
                    for db in range(4):
                        ov_items += [("gu", db, pc, qcf) for pc in range(8)]
                        ov_items += [("dn", db, oc, qcf) for oc in range(8)]
                ov_pos = [0]

                ov_issued = {}
                ov_casted = {}

                def ov_dma(pos):
                    if pos >= len(ov_items) or pos in ov_issued:
                        return
                    kind, db, x, qcf = ov_items[pos]
                    if kind == "gu":
                        wf0_rel[0].acquire(*Qs)
                        gv = ctx.inc("qo0", sync.dma_start(wf0[0], gate_t[0, db, x]))
                        wf0_rel[1].acquire(*Qs)
                        uv = ctx.inc("qo1", sync.dma_start(wf0[1], up_t[0, db, x]))
                        ov_issued[pos] = (gv, uv)
                    else:
                        wf0_rel[0].acquire(*Qs)
                        ov_issued[pos] = ctx.inc("qo2", sync.dma_start(wf0[0], down_t[0, db, x]))

                def ov_cast(pos):
                    if pos >= len(ov_items) or pos in ov_casted:
                        return
                    kind, db, x, qcf = ov_items[pos]
                    wi = x % 2
                    if kind == "gu":
                        gv, uv = ov_issued[pos]
                        wg0_rel[wi].acquire(*Vs)
                        W(Vs, "qo0", gv)
                        cg = ctx.inc("dve", v.tensor_copy(wg0[wi], wf0[0]))
                        wf0_rel[0].release("dve", cg)
                        wu0_rel[wi].acquire(*Vs)
                        W(Vs, "qo1", uv)
                        cu = ctx.inc("dve", v.tensor_copy(wu0[wi], wf0[1]))
                        wf0_rel[1].release("dve", cu)
                        ov_casted[pos] = (cg, cu)
                    else:
                        dv = ov_issued[pos]
                        wd0_rel[wi].acquire(*Vs)
                        W(Vs, "qo2", dv)
                        cd = ctx.inc("dve", v.tensor_copy(wd0[wi], wf0[0]))
                        wf0_rel[0].release("dve", cd)
                        ov_casted[pos] = cd

                def ov_gu(pos, db, pc, qcf):
                    wi = pc % 2
                    cg, cu = ov_casted[pos]
                    brel[6].acquire(*Ts)
                    W(Ts, "dve", cg)
                    W(Ts, "dve", ln2_done[qcf])
                    for c8 in range(8):
                        ins = t.matmul(bank(6), wg0[wi][:, c8 * 128:(c8 + 1) * 128],
                                       ln2[:, c8 * T + qcf * 512: c8 * T + (qcf + 1) * 512],
                                       start=(c8 == 0), stop=(c8 == 7))
                    mg = ctx.inc("pe", ins)
                    brel[7].acquire(*Ts)
                    W(Ts, "dve", cu)
                    for c8 in range(8):
                        ins = t.matmul(bank(7), wu0[wi][:, c8 * 128:(c8 + 1) * 128],
                                       ln2[:, c8 * T + qcf * 512: c8 * T + (qcf + 1) * 512],
                                       start=(c8 == 0), stop=(c8 == 7))
                    mu = ctx.inc("pe", ins)
                    wg0_rel[wi].release("pe", mg)
                    wu0_rel[wi].release("pe", mu)
                    silg1_rel.acquire(*Ss)
                    W(Ss, "pe", mg)
                    sg = ctx.inc("act", s.activation(silg1, bank(6), AF.Exp, scale=-1.0))
                    W(Vs, "act", sg)
                    ctx.inc("dve", v.tensor_scalar_add(silg1, silg1, 1.0))
                    W(Vs, "dve", ctx.val("dve"))
                    ctx.inc("dve", v.reciprocal(silg1b, silg1))
                    W(Vs, "dve", ctx.val("dve"))
                    sv6 = ctx.inc("dve", v.tensor_tensor(silg1b, bank(6), silg1b, ALU.mult))
                    brel[6].release("dve", sv6)
                    silg1_rel.release("dve", sv6)
                    hq_rel[pc].acquire(*Vs)
                    W(Vs, "pe", mu)
                    hv = ctx.inc("dve", v.tensor_tensor(
                        hq[:, pc * 512:(pc + 1) * 512], bank(7), silg1b, ALU.mult))
                    brel[7].release("dve", hv)
                    hq_done[pc] = hv

                def ov_dn(pos, db, oc, qcf):
                    wi = oc % 2
                    cd = ov_casted[pos]
                    bk = 6 + oc % 2
                    brel[bk].acquire(*Ts)
                    W(Ts, "dve", cd)
                    W(Ts, "dve", max(hq_done))
                    for pcx in range(8):
                        ins = t.matmul(bank(bk), wd0[wi][:, pcx * 128:(pcx + 1) * 128],
                                       hq[:, pcx * 512:(pcx + 1) * 512],
                                       start=(pcx == 0), stop=(pcx == 7))
                    md = ctx.inc("pe", ins)
                    wd0_rel[wi].release("pe", md)
                    W(Vs, "pe", md)
                    W(Vs, "act", p3_act[qcf])
                    W(Vs, "dve", max(xr_last0[oc][qcf], xr_dve[oc][qcf]))
                    dvx = ctx.inc("dve", v.tensor_tensor(
                        XR[:, oc * T + qcf * 512: oc * T + (qcf + 1) * 512],
                        XR[:, oc * T + qcf * 512: oc * T + (qcf + 1) * 512],
                        bank(bk), ALU.add))
                    xr_last0[oc][qcf] = dvx
                    brel[bk].release("dve", dvx)
                    if oc == 7:
                        for pc in range(8):
                            hq_rel[pc].release("pe", md)

                def ov_emit(n):
                    while n > 0 and ov_pos[0] < len(ov_items):
                        pos = ov_pos[0]
                        kind, db, x, qcf = ov_items[pos]
                        if qcf >= chunks_done[0]:
                            return
                        ov_dma(pos)
                        ov_cast(pos)
                        ov_dma(pos + 1)
                        if kind == "gu":
                            ov_gu(pos, db, x, qcf)
                        else:
                            ov_dn(pos, db, x, qcf)
                        ov_cast(pos + 1)
                        ov_pos[0] += 1
                        n -= 1

                def p3_chunk(qc):
                    for h in range(2):
                        col0 = qc * 512 + h * 256
                        brel[7].acquire(*Ts)
                        for c8 in range(8):
                            ti = 0
                            xbt3_rel[ti].acquire(*Vs)
                            W(Vs, "dve", xr_dve[c8][qc])
                            cv = ctx.inc("dve", v.tensor_copy(
                                xbt3[ti], XR[:, c8 * T + col0: c8 * T + col0 + 256]))
                            W(Ts, "dve", cv)
                            mv = ctx.inc("pe", t.matmul(
                                bank(7)[0:1, 0:256], ones_bf[:, 0:1], xbt3[ti],
                                start=(c8 == 0), stop=(c8 == 7)))
                            xbt3_rel[ti].release("pe", mv)
                        for c8 in range(8):
                            ti = 0
                            xst3_rel[ti].acquire(*Ss)
                            W(Ss, "dve", xr_dve[c8][qc])
                            sv = ctx.inc("act", s.activation(
                                xst3[ti], XR[:, c8 * T + col0: c8 * T + col0 + 256],
                                AF.Square))
                            W(Ts, "act", sv)
                            mq = ctx.inc("pe", t.matmul(
                                bank(7)[0:1, 256:512], ones_bf[:, 0:1], xst3[ti],
                                start=(c8 == 0), stop=(c8 == 7)))
                            xst3_rel[ti].release("pe", mq)
                        W(Vs, "pe", mq)
                        dv1 = ctx.inc("dve", v.tensor_scalar_mul(
                            srow[0:1, 0:256], bank(7)[0:1, 0:256], 1.0 / D))
                        dv2 = ctx.inc("dve", v.tensor_scalar_mul(
                            srow[0:1, 512:768], bank(7)[0:1, 256:512], 1.0 / D))
                        W(Vs, "dve", dv2)
                        ctx.inc("dve", v.tensor_mul(srow[0:1, 1024:1280],
                                                    srow[0:1, 0:256], srow[0:1, 0:256]))
                        W(Vs, "dve", ctx.val("dve"))
                        dvv = ctx.inc("dve", v.tensor_sub(
                            srow[0:1, 1024:1280], srow[0:1, 512:768], srow[0:1, 1024:1280]))
                        W(Ss, "dve", dvv)
                        ctx.inc("act", s.activation(srow[0:1, 512:768], srow[0:1, 1024:1280],
                                                    AF.Ln, bias=eps_t[0:1, 0:1], scale=1.0))
                        W(Ss, "act", ctx.val("act"))
                        sv2 = ctx.inc("act", s.activation(srow[0:1, 512:768],
                                                          srow[0:1, 512:768],
                                                          AF.Exp, scale=-0.5))
                        W(Vs, "act", sv2)
                        dvn = ctx.inc("dve", v.scalar_tensor_tensor(
                            srow[0:1, 1024:1280], srow[0:1, 0:256], -1.0,
                            srow[0:1, 512:768], ALU.mult, ALU.mult))
                        W(Ts, "act", sv2)
                        W(Ts, "dve", dvn)
                        t.matmul(bank(7)[:, 0:256], ones_f, srow[0:1, 512:768],
                                 start=True, stop=True)
                        mvb = ctx.inc("pe", t.matmul(
                            bank(7)[:, 256:512], ones_f, srow[0:1, 1024:1280],
                            start=True, stop=True))
                        W(Vs, "pe", mvb)
                        lvh = 0
                        for c8 in range(8):
                            ni = 0
                            nrm3_rel[ni].acquire(*Vs)
                            dvt = ctx.inc("dve", v.tensor_tensor(
                                nrm3[ni], XR[:, c8 * T + col0: c8 * T + col0 + 256],
                                bank(7)[:, 0:256], ALU.mult))
                            W(Vs, "dve", dvt)
                            lvh = ctx.inc("dve", v.tensor_tensor(
                                ln2[:, c8 * T + col0: c8 * T + col0 + 256],
                                nrm3[ni], bank(7)[:, 256:512], ALU.add))
                            nrm3_rel[ni].release("dve", lvh)
                        brel[7].release("dve", lvh)
                        brel[7].acquire(*Ts)
                        W(Ts, "qsw", w_small)
                        W(Ts, "dve", lvh)
                        for c8 in range(8):
                            ins = t.matmul(bank(7)[0:7, 0:256],
                                           wr_b[:, c8 * 7:(c8 + 1) * 7],
                                           ln2[:, c8 * T + col0: c8 * T + col0 + 256],
                                           start=(c8 == 0), stop=(c8 == 7))
                        mlg = ctx.inc("pe", ins)
                        W(Ss, "pe", mlg)
                        slg = ctx.inc("act", s.activation(
                            lg[0:7, h * 256:(h + 1) * 256], bank(7)[0:7, 0:256], AF.Copy))
                        brel[7].release("act", slg)
                    ln2_done[qc] = lvh
                    brel[7].acquire(*Ts)
                    W(Ts, "act", slg)
                    W(Ts, "pool", ident_done)
                    for tt in range(4):
                        ins = t.transpose(PS[:, 7 * 512 + tt * 7: 7 * 512 + (tt + 1) * 7],
                                          lg[0:7, tt * 128:(tt + 1) * 128], ident[0:7, 0:7])
                    mtr = ctx.inc("pe", ins)
                    W(Vs, "pe", mtr)
                    dtr = ctx.inc("dve", v.tensor_copy(lgT, PS[:, 7 * 512: 7 * 512 + 28]))
                    brel[7].release("dve", dtr)

                    v3 = lambda a: a.rearrange("p (t j) -> p t j", j=7)
                    bc = lambda a: a.unsqueeze(2).broadcast_to([128, 4, 7])

                    def chain(ins):
                        r = ctx.inc("dve", ins)
                        W(Vs, "dve", r)
                        return r
                    W(Vs, "dve", dtr)
                    chain(v.reduce_max(st1, v3(lgT), axis=AX.X))
                    chain(v.tensor_tensor(v3(pr), v3(lgT), bc(st1), ALU.subtract))
                    W(Ss, "dve", ctx.val("dve"))
                    spr = ctx.inc("act", s.activation(pr, pr, AF.Exp))
                    W(Vs, "act", spr)
                    chain(v.reduce_sum(st2, v3(pr), axis=AX.X))
                    chain(v.reciprocal(st2, st2))
                    chain(v.tensor_tensor(v3(pr), v3(pr), bc(st2), ALU.mult))
                    chain(v.reduce_max(st1, v3(pr), axis=AX.X))
                    chain(v.tensor_tensor(v3(eq1), v3(pr), bc(st1), ALU.is_equal))
                    chain(v.tensor_scalar(eq2, eq1, -1.0, 1.0, ALU.mult, ALU.add))
                    chain(v.tensor_tensor(pr, pr, eq2, ALU.mult))
                    chain(v.reduce_max(st2, v3(pr), axis=AX.X))
                    chain(v.tensor_tensor(v3(eq2), v3(pr), bc(st2), ALU.is_equal))
                    chain(v.tensor_tensor(st3, st1, st2, ALU.add))
                    chain(v.tensor_scalar_add(st3, st3, 1e-9))
                    chain(v.reciprocal(st3, st3))
                    chain(v.tensor_tensor(st4, st1, st3, ALU.mult))
                    chain(v.tensor_tensor(st3, st2, st3, ALU.mult))
                    chain(v.tensor_tensor(v3(eq1), v3(eq1), bc(st4), ALU.mult))
                    chain(v.tensor_tensor(v3(eq2), v3(eq2), bc(st3), ALU.mult))
                    dcomb = ctx.inc("dve", v.tensor_tensor(combT, eq1, eq2, ALU.add))

                    brel[7].acquire(*Ts)
                    W(Ts, "dve", dcomb)
                    for tt in range(4):
                        ins = t.transpose(PS[0:7, 7 * 512 + tt * 128: 7 * 512 + (tt + 1) * 128],
                                          combT[:, tt * 7:(tt + 1) * 7], ident)
                    mtr2 = ctx.inc("pe", ins)
                    W(Vs, "pe", mtr2)
                    dcp = ctx.inc("dve", v.tensor_copy(
                        comb[0:7, qc * 512:(qc + 1) * 512], PS[0:7, 7 * 512:8 * 512]))
                    brel[7].release("dve", dcp)
                    comb_done[qc] = dcp
                    p3_act[qc] = ctx.val("act")

                for qc in range(QC):
                    for hp in range(NP):
                        qcol = slice(hp * T + qc * 512, hp * T + (qc + 1) * 512)
                        brel[4].acquire(*Ts)
                        brel[5].acquire(*Ts)
                        W(Ts, "dve", p1_dve)
                        W(Ts, "act", p1_act)
                        pv = 0
                        exp_of = {}

                        def issue_scores(kt):
                            sb = 2 * (kt % 2)
                            brel[sb].acquire(*Ts)
                            brel[sb + 1].acquire(*Ts)
                            t.matmul(bank(sb), K2[0:64, kt * 128:(kt + 1) * 128],
                                     Q[0:64, qcol], start=True, stop=True,
                                     tile_position=(0, 0))
                            ins = t.matmul(bank(sb + 1), K2[64:128, kt * 128:(kt + 1) * 128],
                                           Q[64:128, qcol], start=True, stop=True,
                                           tile_position=(64, 0))
                            mv = ctx.inc("pe", ins)
                            pt = kt % NPT
                            PT_rel[pt].acquire(*Ss)
                            W(Ss, "pe", mv)
                            W(Ss, "dve", bias_done)
                            ev = ctx.inc("act", s.activation(
                                PT[pt], PS[:, sb * 512: sb * 512 + 1024], AF.Exp,
                                bias=bias_hd[:, hp:hp + 1], scale=0.125))
                            brel[sb].release("act", ev)
                            brel[sb + 1].release("act", ev)
                            exp_of[kt] = ev

                        issue_scores(0)
                        issue_scores(1)
                        for kt in range(KT):
                            if kt + 2 < KT:
                                issue_scores(kt + 2)
                            pt = kt % NPT
                            W(Ts, "act", exp_of[kt])
                            t.matmul(bank(4), VA[:, kt * 65:kt * 65 + 128],
                                     PT[pt][:, 0:512], start=(kt == 0), stop=(kt == KT - 1))
                            pv = ctx.inc("pe", t.matmul(
                                bank(5), VA[:, kt * 65:kt * 65 + 128],
                                PT[pt][:, 512:1024], start=(kt == 0), stop=(kt == KT - 1)))
                            PT_rel[pt].release("pe", pv)
                            if kt % 8 == 7 and kt < KT - 1:
                                ov_emit(1)
                        rd_rel.acquire(*Ss)
                        W(Ss, "pe", pv)
                        s.activation(rdrow2[0:1, 0:512], PS[64:65, 4 * 512:5 * 512], AF.Ln)
                        lnv = ctx.inc("act", s.activation(
                            rdrow2[0:1, 512:1024], PS[64:65, 5 * 512:6 * 512], AF.Ln))
                        W(Ss, "act", lnv)
                        rde = ctx.inc("act", s.activation(rdrow2, rdrow2, AF.Exp, scale=-1.0))
                        brel[6].acquire(*Ts)
                        W(Ts, "act", rde)
                        t.matmul(PS[0:64, 6 * 512:7 * 512], ones_f[0:1, 0:64],
                                 rdrow2[0:1, 0:512], start=True, stop=True)
                        mb = ctx.inc("pe", t.matmul(
                            PS[64:128, 6 * 512:7 * 512], ones_f[0:1, 0:64],
                            rdrow2[0:1, 512:1024], start=True, stop=True))
                        rd_rel.release("pe", mb)
                        a2 = 0
                        for hh in range(2):
                            rows = slice(64 * hh, 64 * hh + 64)
                            ab = 4 + hh
                            W(Ss, "dve", a2_prev[0])
                            sc = ctx.inc("act", s.activation(
                                att_s[rows, :], PS[0:64, ab * 512:(ab + 1) * 512], AF.Copy))
                            brel[ab].release("act", sc)
                            W(Vs, "act", sc)
                            W(Vs, "pe", mb)
                            m1 = ctx.inc("dve", v.tensor_tensor(
                                att_s[rows, :], att_s[rows, :],
                                PS[rows, 6 * 512:7 * 512], ALU.mult))
                            W(Vs, "qxr", xr_load)
                            a2 = ctx.inc("dve", v.tensor_tensor(
                                XR[rows, hp * T + qc * 512: hp * T + (qc + 1) * 512],
                                XR[rows, hp * T + qc * 512: hp * T + (qc + 1) * 512],
                                att_s[rows, :], ALU.add))
                        brel[6].release("dve", a2)
                        a2_prev[0] = a2
                        xr_dve[hp][qc] = a2
                        ov_emit(1)
                    p3_chunk(qc)
                    chunks_done[0] = qc + 1
                    ov_emit(2)

                ov_emit(len(ov_items))
                assert ov_pos[0] == len(ov_items)
                tap("X2", [128, 8 * T], XR, "dve")
                tap("ln2", [128, 4 * T], ln2.bitcast(F32), "dve")
                tap("comb", [7, T], comb, "dve")
                ctx.barrier(ALL_ENG, exclude=("qpc",))
            ctx.barrier(ALL_ENG, exclude=("qpc",))

        with ExitStack() as fb:
            fb.enter_context(nc.named_scope("P4"))
            SB4 = lambda n, sh, dt=F32: fb.enter_context(nc.sbuf_tensor(n, sh, dt)).ap()
            silg = SB4("silg", [128, T])
            silg_rel = Rel(ctx)
            hsb = SB4("hsb", [128, 8 * T], BF16)
            hsb_rel = [Rel(ctx) for _ in range(8)]
            cbs = SB4("cbs", [128, T])
            cbs_rel = Rel(ctx)
            crow = SB4("crow", [1, T])
            crow_rel = Rel(ctx)
            wgb = [SB4(f"wgb{i}", [128, 8 * 128], BF16) for i in range(2)]
            wgb_rel = [Rel(ctx) for _ in range(2)]
            wub = [SB4(f"wub{i}", [128, 8 * 128], BF16) for i in range(2)]
            wub_rel = [Rel(ctx) for _ in range(2)]
            wdb = [SB4(f"wdb{i}", [128, 8 * 128], BF16) for i in range(2)]
            wdb_rel = [Rel(ctx) for _ in range(2)]
            wpf = [SB4(f"wpf{i}", [128, 1024]) for i in range(2)]
            wpf_rel = [Rel(ctx) for _ in range(2)]

            po_items = []
            for e in range(1, 8):
                for db in range(4):
                    po_items += [("gu", e, db, pc) for pc in range(8)]
                    po_items += [("dn", e, db, oc) for oc in range(8)]
            po_issued = {}
            po_casted = {}

            def po_dma(pos):
                if pos >= len(po_items) or pos in po_issued:
                    return
                kind, e, db, x = po_items[pos]
                if kind == "gu":
                    wpf_rel[0].acquire(*Qs)
                    gv = ctx.inc("qw0", sync.dma_start(wpf[0], gate_t[e, db, x]))
                    wpf_rel[1].acquire(*Qs)
                    uv = ctx.inc("qw1", sync.dma_start(wpf[1], up_t[e, db, x]))
                    po_issued[pos] = (gv, uv)
                else:
                    wpf_rel[0].acquire(*Qs)
                    po_issued[pos] = ctx.inc("qw2", sync.dma_start(wpf[0], down_t[e, db, x]))

            def po_cast(pos):
                if pos >= len(po_items) or pos in po_casted:
                    return
                kind, e, db, x = po_items[pos]
                wi = x % 2
                if kind == "gu":
                    gv, uv = po_issued[pos]
                    wgb_rel[wi].acquire(*Vs)
                    W(Vs, "qw0", gv)
                    cg = ctx.inc("dve", v.tensor_copy(wgb[wi], wpf[0]))
                    wpf_rel[0].release("dve", cg)
                    wub_rel[wi].acquire(*Vs)
                    W(Vs, "qw1", uv)
                    cu = ctx.inc("dve", v.tensor_copy(wub[wi], wpf[1]))
                    wpf_rel[1].release("dve", cu)
                    po_casted[pos] = (cg, cu)
                else:
                    dv = po_issued[pos]
                    wdb_rel[wi].acquire(*Vs)
                    W(Vs, "qw2", dv)
                    cd = ctx.inc("dve", v.tensor_copy(wdb[wi], wpf[0]))
                    wpf_rel[0].release("dve", cd)
                    po_casted[pos] = cd

            dcp4 = comb_done[QC - 1]
            xr_last = [[max(xr_dve[oc][qc], xr_last0[oc][qc]) for qc in range(QC)]
                       for oc in range(8)]
            ln2_all = max(ln2_done)
            pos = [0]
            for e in range(1, 8):
                if e > 0:
                    crow_rel.acquire(*Qs)
                    W(Qs, "dve", dcp4)
                    cr = ctx.inc("qg", sync.dma_start(crow, comb[e - 1:e, :]))
                    cbs_rel.acquire(*Vs)
                    for qc in range(QC):
                        bk = 2 + qc % 2
                        brel[bk].acquire(*Ts)
                        W(Ts, "qg", cr)
                        ins = t.matmul(bank(bk), ones_f,
                                       crow[0:1, qc * 512:(qc + 1) * 512],
                                       start=True, stop=True)
                        mv = ctx.inc("pe", ins)
                        W(Vs, "pe", mv)
                        dv = ctx.inc("dve", v.tensor_copy(
                            cbs[:, qc * 512:(qc + 1) * 512], bank(bk)))
                        brel[bk].release("dve", dv)
                    crow_rel.release("pe", ctx.val("pe"))
                    cb_done = ctx.val("dve")
                else:
                    cb_done = 0

                for db in range(4):
                    h_done = 0
                    for pc in range(8):
                        wi = pc % 2
                        po_dma(pos[0])
                        po_cast(pos[0])
                        po_dma(pos[0] + 1)
                        cg, cu = po_casted[pos[0]]
                        for qc in range(QC):
                            brel[qc].acquire(*Ts)
                        W(Ts, "dve", cg)
                        W(Ts, "dve", ln2_all)
                        ins = None
                        for qc in range(QC):
                            for c8 in range(8):
                                ins = t.matmul(
                                    bank(qc), wgb[wi][:, c8 * 128:(c8 + 1) * 128],
                                    ln2[:, c8 * T + qc * 512: c8 * T + (qc + 1) * 512],
                                    start=(c8 == 0), stop=(c8 == 7))
                        mg = ctx.inc("pe", ins)
                        for qc in range(QC):
                            brel[4 + qc].acquire(*Ts)
                        W(Ts, "dve", cu)
                        for qc in range(QC):
                            for c8 in range(8):
                                ins = t.matmul(
                                    bank(4 + qc), wub[wi][:, c8 * 128:(c8 + 1) * 128],
                                    ln2[:, c8 * T + qc * 512: c8 * T + (qc + 1) * 512],
                                    start=(c8 == 0), stop=(c8 == 7))
                        mu = ctx.inc("pe", ins)
                        wgb_rel[wi].release("pe", mg)
                        wub_rel[wi].release("pe", mu)
                        po_cast(pos[0] + 1)
                        pos[0] += 1
                        silg_rel.acquire(*Ss)
                        W(Ss, "pe", mg)
                        ssg = ctx.inc("act", s.activation(silg, PS[:, 0:2048], AF.Silu))
                        for qc in range(QC):
                            brel[qc].release("act", ssg)
                        hsb_rel[pc].acquire(*Vs)
                        W(Vs, "act", ssg)
                        W(Vs, "pe", mu)
                        if e > 0:
                            W(Vs, "dve", cb_done)
                            ctx.inc("dve", v.tensor_tensor(silg, PS[:, 2048:4096],
                                                           silg, ALU.mult))
                            W(Vs, "dve", ctx.val("dve"))
                            hv = ctx.inc("dve", v.tensor_tensor(
                                hsb[:, pc * T:(pc + 1) * T], silg, cbs, ALU.mult))
                        else:
                            hv = ctx.inc("dve", v.tensor_tensor(
                                hsb[:, pc * T:(pc + 1) * T], PS[:, 2048:4096],
                                silg, ALU.mult))
                        silg_rel.release("dve", hv)
                        for qc in range(QC):
                            brel[4 + qc].release("dve", hv)
                        h_done = hv
                    for oc in range(8):
                        wi = oc % 2
                        po_dma(pos[0])
                        po_cast(pos[0])
                        po_dma(pos[0] + 1)
                        cd = po_casted[pos[0]]
                        first = True
                        for qc in range(QC):
                            bk = qc
                            brel[bk].acquire(*Ts)
                            W(Ts, "dve", cd)
                            W(Ts, "dve", h_done)
                            ins = None
                            for pcx in range(8):
                                ins = t.matmul(
                                    bank(bk), wdb[wi][:, pcx * 128:(pcx + 1) * 128],
                                    hsb[:, pcx * T + qc * 512: pcx * T + (qc + 1) * 512],
                                    start=(pcx == 0), stop=(pcx == 7))
                            md = ctx.inc("pe", ins)
                            W(Vs, "pe", md)
                            W(Vs, "dve", xr_last[oc][qc])
                            dvx = ctx.inc("dve", v.tensor_tensor(
                                XR[:, oc * T + qc * 512: oc * T + (qc + 1) * 512],
                                XR[:, oc * T + qc * 512: oc * T + (qc + 1) * 512],
                                bank(bk), ALU.add))
                            xr_last[oc][qc] = dvx
                            brel[bk].release("dve", dvx)
                            if e == 7 and db == 3:
                                W(Qs, "dve", dvx)
                                ctx.inc("st", sync.dma_start(
                                    out[oc * 128:(oc + 1) * 128, qc * 512:(qc + 1) * 512],
                                    XR[:, oc * T + qc * 512: oc * T + (qc + 1) * 512]))
                            if first:
                                first = False
                                po_cast(pos[0] + 1)
                        wdb_rel[wi].release("pe", ctx.val("pe"))
                        pos[0] += 1
                    pe_after_down = ctx.val("pe")
                    for pc in range(8):
                        hsb_rel[pc].release("pe", pe_after_down)
                cbs_rel.release("dve", ctx.val("dve"))

            sync.wait_ge(ctx.sems["st"][0], ctx.val("st"))
    return nc, tap_aps



def host_inputs(inputs, core):
    b, half = core // 2, core % 2
    x = np.asarray(inputs["L0_attn_x"])
    xT = np.ascontiguousarray(x[b].T)
    x_own = np.ascontiguousarray(xT[:, half * T:(half + 1) * T])
    x_oth = np.ascontiguousarray(xT[:, (1 - half) * T:(2 - half) * T])
    x_perm = np.ascontiguousarray(np.concatenate([x_oth, x_own], axis=1))

    def tile_in_by_out(w):
        wt = np.ascontiguousarray(np.asarray(w).T)
        o = w.shape[0]
        return np.ascontiguousarray(
            wt.reshape(8, 128, o).transpose(1, 0, 2).reshape(128, 8 * o))

    wq_h = tile_in_by_out(inputs["L0_attn_Wq"])
    wk_h = tile_in_by_out(inputs["L0_attn_Wk"])
    wv_h = tile_in_by_out(inputs["L0_attn_Wv"])
    wr_h = tile_in_by_out(inputs["L0_ffn_router_w"])

    def tile_gateup(w):
        wt = np.asarray(w).T
        r = wt.reshape(8, 128, 4, 8, 128)
        return np.ascontiguousarray(r.transpose(2, 3, 1, 0, 4).reshape(4, 8, 128, 1024))

    def tile_down(w):
        wt = np.asarray(w).T
        r = wt.reshape(4, 8, 128, 8, 128)
        return np.ascontiguousarray(r.transpose(0, 3, 2, 1, 4).reshape(4, 8, 128, 1024))

    gate_h = np.stack([tile_gateup(inputs["L0_ffn_shared_gate"])]
                      + [tile_gateup(np.asarray(inputs["L0_ffn_routed_gate"])[i])
                         for i in range(7)])
    up_h = np.stack([tile_gateup(inputs["L0_ffn_shared_up"])]
                    + [tile_gateup(np.asarray(inputs["L0_ffn_routed_up"])[i])
                       for i in range(7)])
    down_h = np.stack([tile_down(inputs["L0_ffn_shared_down"])]
                      + [tile_down(np.asarray(inputs["L0_ffn_routed_down"])[i])
                         for i in range(7)])

    return dict(x_full=x_perm, x_own=x_own, wq=wq_h, wk=wk_h, wv=wv_h, wr=wr_h,
                gate_t=gate_h, up_t=up_h, down_t=down_h)


def kernel(**inputs):
    nc, _ = build()
    shared = None
    in_maps = []
    for c in range(8):
        m = host_inputs(inputs, c)
        if shared is None:
            shared = {k: m[k] for k in ("wq", "wk", "wv", "wr", "gate_t", "up_t", "down_t")}
        else:
            for k in shared:
                m[k] = shared[k]
        in_maps.append(m)
    res = run_bass_kernel_spmd(nc, in_maps, core_ids=list(range(8)))
    outT = np.zeros((4, D, S), dtype=np.float32)
    for c in range(8):
        b, half = c // 2, c % 2
        outT[b][:, half * T:(half + 1) * T] = res.results[c]["out"]
    return np.ascontiguousarray(outT.transpose(0, 2, 1))


# version 12 (final)
# speedup vs baseline: 1.3128x; 1.1583x over previous
import numpy as np
from contextlib import ExitStack

import concourse.bass as bass
import concourse.mybir as mybir
from concourse.bass_utils import run_bass_kernel_spmd

F32 = mybir.dt.float32
BF16 = mybir.dt.bfloat16
AF = mybir.ActivationFunctionType
ALU = mybir.AluOpType
AX = mybir.AxisListType

D = 1024
S = 4096
T = 2048
NP = 8
EPS = 1e-5
KT = S // 128
QC = T // 512
CW = 256
NPT = 4


class Ctx:
    def __init__(self, nc, stack):
        self.nc = nc
        self.stack = stack
        self.sems = {}
        self.hwm = {}

    def sem(self, name, step=1):
        h = self.stack.enter_context(self.nc.semaphore(name))
        self.sems[name] = [h, step, 0]

    def inc(self, name, inst):
        e = self.sems[name]
        inst.then_inc(e[0], e[1])
        e[2] += e[1]
        return e[2]

    def val(self, name):
        return self.sems[name][2]

    def wait(self, ekey, eng, name, v):
        if v <= 0:
            return
        key = (ekey, name)
        if self.hwm.get(key, 0) >= v:
            return
        eng.wait_ge(self.sems[name][0], v)
        self.hwm[key] = v

    def barrier(self, engines, exclude=()):
        for ekey, eng in engines:
            for name in self.sems:
                if name in exclude:
                    continue
                self.wait(ekey, eng, name, self.val(name))


class Rel:
    def __init__(self, ctx):
        self.ctx = ctx
        self.rel = []

    def acquire(self, ekey, eng):
        for (sn, vv) in self.rel:
            self.ctx.wait(ekey, eng, sn, vv)
        self.rel = []

    def release(self, sem_name, val):
        self.rel.append((sem_name, val))


def build(taps=()):
    nc = bass.Bass("TRN2", target_bir_lowering=False)
    dr = lambda n, sh, kind: nc.dram_tensor(n, sh, F32, kind=kind).ap()

    x_full = dr("x_full", [D, S], "ExternalInput")
    x_own = dr("x_own", [D, T], "ExternalInput")
    wq = dr("wq", [128, 8 * 1024], "ExternalInput")
    wk = dr("wk", [128, 8 * 64], "ExternalInput")
    wv = dr("wv", [128, 8 * 64], "ExternalInput")
    wr = dr("wr", [128, 8 * 7], "ExternalInput")
    gate_t = dr("gate_t", [8, 4, 8, 128, 1024], "ExternalInput")
    up_t = dr("up_t", [8, 4, 8, 128, 1024], "ExternalInput")
    down_t = dr("down_t", [8, 4, 8, 128, 1024], "ExternalInput")
    out = dr("out", [D, T], "ExternalOutput")
    gate_b = nc.dram_tensor("gate_b", [8, 4, 8, 128, 1024], BF16, kind="Internal").ap()
    up_b = nc.dram_tensor("up_b", [8, 4, 8, 128, 1024], BF16, kind="Internal").ap()
    down_b = nc.dram_tensor("down_b", [8, 4, 8, 128, 1024], BF16, kind="Internal").ap()

    tap_aps = {}

    with ExitStack() as stack:
        ctx = Ctx(nc, stack)
        SB = lambda n, sh, dt=F32: stack.enter_context(nc.sbuf_tensor(n, sh, dt)).ap()
        sync, v, s, t, g = nc.sync, nc.vector, nc.scalar, nc.tensor, nc.gpsimd
        Ts, Ss, Vs, Gs, Qs = ("t", t), ("s", s), ("v", v), ("g", g), ("q", sync)
        ALL_ENG = [Ts, Ss, Vs, Gs, Qs]

        for name, step in ([("qg", 16), ("qxr", 16), ("qwq", 16), ("qsw", 16), ("qpc", 16),
                            ("pe", 1), ("act", 1),
                            ("dve", 1), ("pool", 1), ("st", 16)]
                           + [(f"qw{i}", 16) for i in range(6)]
                           + [(f"qxf{i}", 16) for i in range(2)]):
            ctx.sem(name, step)
        W = lambda ee, name, val: ctx.wait(ee[0], ee[1], name, val)

        def tap(name, shape, src_ap, dep_sem):
            if name not in taps:
                return
            ap = dr("tap_" + name, shape, "ExternalOutput")
            tap_aps[name] = ap
            W(Qs, dep_sem, ctx.val(dep_sem))
            ctx.inc("st", sync.dma_start(ap, src_ap))

        XR = SB("XR", [128, 8 * T])
        ln2 = SB("ln2", [128, 8 * T], BF16)
        comb = SB("comb", [7, T])
        lg = SB("lg", [7, 512])
        lgT = SB("lgT", [128, 4 * 7])
        st1 = SB("st1", [128, 4]); st2 = SB("st2", [128, 4])
        st3 = SB("st3", [128, 4]); st4 = SB("st4", [128, 4])
        pr = SB("pr", [128, 28]); eq1 = SB("eq1", [128, 28])
        eq2 = SB("eq2", [128, 28]); combT = SB("combT", [128, 28])
        PS = stack.enter_context(nc.psum_tensor("PS", [128, 4096], F32)).ap()
        bank = lambda i: PS[:, i * 512:(i + 1) * 512]
        brel = [Rel(ctx) for _ in range(8)]

        wk_b = SB("wk_b", [128, 8 * 64], BF16)
        wv_b = SB("wv_b", [128, 8 * 64], BF16)
        wr_b = SB("wr_b", [128, 8 * 7], BF16)
        ones_bf = SB("ones_bf", [128, 2], BF16)
        ones_f = SB("ones_f", [1, 128])
        eps_t = SB("eps_t", [1, 2])
        ident = SB("ident", [128, 128])
        srow = SB("srow", [1, 3 * 512])
        mrow = SB("mrow", [1, 16])
        arow = SB("arow", [1, 16])

        ctx.inc("dve", v.memset(ones_bf, 1.0))
        ctx.inc("dve", v.memset(ones_f, 1.0))
        consts_dve = ctx.inc("dve", v.memset(eps_t, EPS))
        ctx.inc("pool", g.memset(ident, 1.0))
        W(Gs, "pool", ctx.val("pool"))
        ident_done = ctx.inc("pool", g.affine_select(
            ident, ident, [[1, 128]], ALU.is_equal, 0.0, base=0, channel_multiplier=-1))

        ctx.inc("qsw", g.dma_start(wk_b, wk))
        ctx.inc("qsw", g.dma_start(wv_b, wv))
        w_small = ctx.inc("qsw", g.dma_start(wr_b, wr))
        wq_done = None

        xr_load = ctx.inc("qxr", sync.dma_start(
            XR.rearrange("p (c n) -> p c n", c=8),
            x_own.rearrange("(c p) n -> p c n", p=128)))

        with ExitStack() as fa:
            SBA = lambda n, sh, dt=F32: fa.enter_context(nc.sbuf_tensor(n, sh, dt)).ap()
            Q = SBA("Q", [128, 8 * T], BF16)
            K2 = SBA("K2", [128, S], BF16)
            VA = SBA("VA", [128, KT * 65 + 64], BF16)
            va_ones = ctx.inc("dve", v.memset(VA, 1.0))

            with ExitStack() as fa1:
                fa1.enter_context(nc.named_scope("P1"))
                SB1 = lambda n, sh, dt=F32: fa1.enter_context(nc.sbuf_tensor(n, sh, dt)).ap()
                wq_b = SB1("wq_b", [128, 8 * 1024], BF16)
                xf = [SB1(f"xf{i}", [128, 8 * CW]) for i in range(2)]
                xf_rel = [Rel(ctx) for _ in range(2)]
                xbt = [SB1(f"xbt{i}", [128, CW], BF16) for i in range(2)]
                xbt_rel = [Rel(ctx) for _ in range(2)]
                xst = [SB1(f"xst{i}", [128, CW], BF16) for i in range(2)]
                xst_rel = [Rel(ctx) for _ in range(2)]
                lnb = [SB1(f"lnb{i}", [128, 8 * CW], BF16) for i in range(2)]
                lnb_rel = [Rel(ctx) for _ in range(2)]
                nrm = [SB1(f"nrm{i}", [128, CW]) for i in range(2)]
                nrm_rel = [Rel(ctx) for _ in range(2)]

                wq_done = ctx.inc("qwq", g.dma_start(wq_b, wq))

                xr_src = x_full.rearrange("(c p) n -> p c n", p=128)
                for j in range(16):
                    bi = j % 2
                    own = j >= 8
                    xf_rel[bi].acquire(*Qs)
                    dv = ctx.inc(f"qxf{bi}", sync.dma_start(
                        xf[bi].rearrange("p (c n) -> p c n", c=8),
                        xr_src[:, :, j * CW:(j + 1) * CW]))
                    brel[0].acquire(*Ts)
                    W(Ts, "dve", consts_dve)
                    for c8 in range(8):
                        ti = c8 % 2
                        xbt_rel[ti].acquire(*Vs)
                        W(Vs, f"qxf{bi}", dv)
                        cv = ctx.inc("dve", v.tensor_copy(
                            xbt[ti], xf[bi][:, c8 * CW:(c8 + 1) * CW]))
                        W(Ts, "dve", cv)
                        mv = ctx.inc("pe", t.matmul(
                            bank(0)[0:1, 0:CW], ones_bf[:, 0:1], xbt[ti],
                            start=(c8 == 0), stop=(c8 == 7)))
                        xbt_rel[ti].release("pe", mv)
                    mv_sum = mv
                    brel[1].acquire(*Ts)
                    for c8 in range(8):
                        ti = c8 % 2
                        xst_rel[ti].acquire(*Ss)
                        W(Ss, f"qxf{bi}", dv)
                        sv = ctx.inc("act", s.activation(
                            xst[ti], xf[bi][:, c8 * CW:(c8 + 1) * CW], AF.Square))
                        W(Ts, "act", sv)
                        mq = ctx.inc("pe", t.matmul(
                            bank(1)[0:1, 0:CW], ones_bf[:, 0:1], xst[ti],
                            start=(c8 == 0), stop=(c8 == 7)))
                        xst_rel[ti].release("pe", mq)
                    mv_sq = mq
                    W(Vs, "pe", mv_sq)
                    dv1 = ctx.inc("dve", v.tensor_scalar_mul(
                        srow[0:1, 0:CW], bank(0)[0:1, 0:CW], 1.0 / D))
                    brel[0].release("dve", dv1)
                    dv2 = ctx.inc("dve", v.tensor_scalar_mul(
                        srow[0:1, 512:512 + CW], bank(1)[0:1, 0:CW], 1.0 / D))
                    W(Vs, "dve", dv2)
                    ctx.inc("dve", v.tensor_mul(srow[0:1, 1024:1024 + CW],
                                                srow[0:1, 0:CW], srow[0:1, 0:CW]))
                    W(Vs, "dve", ctx.val("dve"))
                    dvv = ctx.inc("dve", v.tensor_sub(
                        srow[0:1, 1024:1024 + CW], srow[0:1, 512:512 + CW],
                        srow[0:1, 1024:1024 + CW]))
                    W(Ss, "dve", dvv)
                    W(Ss, "dve", consts_dve)
                    ctx.inc("act", s.activation(srow[0:1, 512:512 + CW],
                                                srow[0:1, 1024:1024 + CW],
                                                AF.Ln, bias=eps_t[0:1, 0:1], scale=1.0))
                    W(Ss, "act", ctx.val("act"))
                    sv2 = ctx.inc("act", s.activation(srow[0:1, 512:512 + CW],
                                                      srow[0:1, 512:512 + CW],
                                                      AF.Exp, scale=-0.5))
                    W(Vs, "act", sv2)
                    dvn = ctx.inc("dve", v.scalar_tensor_tensor(
                        srow[0:1, 1024:1024 + CW], srow[0:1, 0:CW], -1.0,
                        srow[0:1, 512:512 + CW], ALU.mult, ALU.mult))
                    brel[2].acquire(*Ts)
                    W(Ts, "act", sv2)
                    W(Ts, "dve", dvn)
                    t.matmul(bank(1)[:, 0:CW], ones_f, srow[0:1, 512:512 + CW],
                             start=True, stop=True)
                    mvb = ctx.inc("pe", t.matmul(
                        bank(2)[:, 0:CW], ones_f, srow[0:1, 1024:1024 + CW],
                        start=True, stop=True))
                    lnb_rel[bi].acquire(*Vs)
                    W(Vs, "pe", mvb)
                    lv = 0
                    for c8 in range(8):
                        ni = c8 % 2
                        nrm_rel[ni].acquire(*Vs)
                        dvt = ctx.inc("dve", v.tensor_tensor(
                            nrm[ni], xf[bi][:, c8 * CW:(c8 + 1) * CW],
                            bank(1)[:, 0:CW], ALU.mult))
                        W(Vs, "dve", dvt)
                        lv = ctx.inc("dve", v.tensor_tensor(
                            lnb[bi][:, c8 * CW:(c8 + 1) * CW], nrm[ni],
                            bank(2)[:, 0:CW], ALU.add))
                        nrm_rel[ni].release("dve", lv)
                    xf_rel[bi].release("dve", lv)
                    brel[1].release("dve", lv)
                    brel[2].release("dve", lv)

                    brel[3].acquire(*Ts)
                    W(Ts, "qsw", w_small)
                    W(Ts, "dve", lv)
                    for c8 in range(8):
                        ins = t.matmul(bank(3)[0:64, 0:CW], wk_b[:, c8 * 64:(c8 + 1) * 64],
                                       lnb[bi][:, c8 * CW:(c8 + 1) * CW],
                                       start=(c8 == 0), stop=(c8 == 7))
                    mk = ctx.inc("pe", ins)
                    W(Ss, "pe", mk)
                    s.activation(K2[0:64, j * CW:(j + 1) * CW], bank(3)[0:64, 0:CW], AF.Copy)
                    se = ctx.inc("act", s.activation(K2[64:128, j * CW:(j + 1) * CW],
                                                     bank(3)[0:64, 0:CW], AF.Copy))
                    brel[3].release("act", se)
                    brel[4].acquire(*Ts)
                    for kt4 in range(CW // 128):
                        for c8 in range(8):
                            ins = t.matmul(
                                PS[:, 4 * 512 + kt4 * 64: 4 * 512 + (kt4 + 1) * 64],
                                lnb[bi][:, c8 * CW + kt4 * 128: c8 * CW + (kt4 + 1) * 128],
                                wv_b[:, c8 * 64:(c8 + 1) * 64],
                                start=(c8 == 0), stop=(c8 == 7))
                    mvv = ctx.inc("pe", ins)
                    W(Vs, "pe", mvv)
                    W(Vs, "dve", va_ones)
                    kt0 = j * (CW // 128)
                    dvv2 = ctx.inc("dve", v.tensor_copy(
                        VA[:, kt0 * 65:(kt0 + CW // 128) * 65]
                        .rearrange("p (k n) -> p k n", k=CW // 128)[:, :, 0:64],
                        PS[:, 4 * 512: 4 * 512 + (CW // 128) * 64]
                        .rearrange("p (k n) -> p k n", k=CW // 128)))
                    brel[4].release("dve", dvv2)
                    qlast = mvv
                    if own:
                        jj = j - 8
                        for hp in range(8):
                            bk = 5 + (hp % 2)
                            brel[bk].acquire(*Ts)
                            W(Ts, "qwq", wq_done)
                            for c8 in range(8):
                                ins = t.matmul(
                                    bank(bk)[:, 0:CW],
                                    wq_b[:, c8 * 1024 + hp * 128: c8 * 1024 + (hp + 1) * 128],
                                    lnb[bi][:, c8 * CW:(c8 + 1) * CW],
                                    start=(c8 == 0), stop=(c8 == 7))
                            mq2 = ctx.inc("pe", ins)
                            W(Ss, "pe", mq2)
                            qe = ctx.inc("act", s.activation(
                                Q[:, hp * T + jj * CW: hp * T + (jj + 1) * CW],
                                bank(bk)[:, 0:CW], AF.Copy))
                            brel[bk].release("act", qe)
                        qlast = ctx.val("pe")
                    lnb_rel[bi].release("pe", qlast)

                ctx.barrier(ALL_ENG, exclude=("qpc",))

            p1_act = ctx.val("act")
            p1_dve = ctx.val("dve")
            for e in range(8):
                for (srcw, dstw) in ((gate_t, gate_b), (up_t, up_b), (down_t, down_b)):
                    ctx.inc("qpc", g.dma_start(dstw[e], srcw[e]))
            qpc_all = ctx.val("qpc")
            tap("K2", [128, S // 2], K2.bitcast(F32), "act")
            tap("VA", [128, (KT * 65 + 1) // 2], VA.bitcast(F32), "dve")
            tap("Q", [128, 4 * T], Q.bitcast(F32), "act")

            with ExitStack() as fa2:
                fa2.enter_context(nc.named_scope("P2"))
                SB2 = lambda n, sh, dt=F32: fa2.enter_context(nc.sbuf_tensor(n, sh, dt)).ap()
                PT = [SB2(f"PT{i}", [128, 1024], BF16) for i in range(NPT)]
                PT_rel = [Rel(ctx) for _ in range(NPT)]
                att_s = SB2("att_s", [128, 512])
                bias_hd = SB2("bias_hd", [128, 8])
                rdrow2 = SB2("rdrow2", [1, 1024])
                rd_rel = Rel(ctx)
                xbt3 = [SB2(f"xbt3_{i}", [128, 256], BF16) for i in range(2)]
                xbt3_rel = [Rel(ctx) for _ in range(2)]
                xst3 = [SB2(f"xst3_{i}", [128, 256], BF16) for i in range(2)]
                xst3_rel = [Rel(ctx) for _ in range(2)]
                nrm3 = [SB2(f"nrm3_{i}", [128, 256]) for i in range(2)]
                nrm3_rel = [Rel(ctx) for _ in range(2)]

                with ExitStack() as fp:
                    SBP = lambda n, sh, dt=F32: fp.enter_context(nc.sbuf_tensor(n, sh, dt)).ap()
                    ksq = SBP("ksq", [64, S], BF16)
                    qsq = [SBP(f"qsq{i}", [128, T], BF16) for i in range(2)]
                    qsq_rel = [Rel(ctx) for _ in range(2)]

                    W(Ss, "act", p1_act)
                    kq = ctx.inc("act", s.activation(ksq, K2[0:64, :], AF.Square))
                    for c in range(8):
                        brel[7].acquire(*Ts)
                        W(Ts, "act", kq)
                        ins = t.matmul(bank(7)[0:1, :], ones_bf[0:64, 0:1],
                                       ksq[:, c * 512:(c + 1) * 512], start=True, stop=True)
                        mv = ctx.inc("pe", ins)
                        W(Vs, "pe", mv)
                        dv = ctx.inc("dve", v.reduce_max(mrow[0:1, c:c + 1],
                                                         bank(7)[0:1, :], axis=AX.X))
                        brel[7].release("dve", dv)
                    W(Vs, "dve", ctx.val("dve"))
                    ctx.inc("dve", v.reduce_max(mrow[0:1, 8:9], mrow[0:1, 0:8], axis=AX.X))

                    for hp in range(NP):
                        qi = hp % 2
                        qsq_rel[qi].acquire(*Ss)
                        W(Ss, "act", p1_act)
                        sv = ctx.inc("act", s.activation(qsq[qi], Q[:, hp * T:(hp + 1) * T],
                                                         AF.Square))
                        for qc in range(QC):
                            brel[7].acquire(*Ts)
                            W(Ts, "act", sv)
                            ins = t.matmul(bank(7)[0:1, :], ones_bf[:, 0:1],
                                           qsq[qi][:, qc * 512:(qc + 1) * 512],
                                           start=True, stop=True)
                            mv = ctx.inc("pe", ins)
                            W(Vs, "pe", mv)
                            dv = ctx.inc("dve", v.reduce_max(
                                mrow[0:1, 12 + qc:13 + qc], bank(7)[0:1, :], axis=AX.X))
                            brel[7].release("dve", dv)
                        qsq_rel[qi].release("pe", ctx.val("pe"))
                        W(Vs, "dve", ctx.val("dve"))
                        ctx.inc("dve", v.reduce_max(mrow[0:1, 10:11], mrow[0:1, 12:16],
                                                    axis=AX.X))
                        W(Vs, "dve", ctx.val("dve"))
                        ctx.inc("dve", v.tensor_add(arow[0:1, 0:1], mrow[0:1, 10:11],
                                                    mrow[0:1, 8:9]))
                        W(Vs, "dve", ctx.val("dve"))
                        dvb2 = ctx.inc("dve", v.tensor_scalar_mul(
                            arow[0:1, 0:1], arow[0:1, 0:1], -1.0 / 16.0))
                        brel[6].acquire(*Ts)
                        W(Ts, "dve", dvb2)
                        ins = t.matmul(PS[:, 6 * 512:6 * 512 + 1], ones_f,
                                       arow[0:1, 0:1], start=True, stop=True)
                        mv = ctx.inc("pe", ins)
                        W(Vs, "pe", mv)
                        dvc = ctx.inc("dve", v.tensor_copy(bias_hd[:, hp:hp + 1],
                                                           PS[:, 6 * 512:6 * 512 + 1]))
                        brel[6].release("dve", dvc)
                    bias_done = ctx.val("dve")

                xr_dve = [[0] * QC for _ in range(NP)]
                ln2_done = [0] * QC
                comb_done = [0] * QC
                a2_prev = [0]

                def p3_chunk(qc):
                    for h in range(2):
                        col0 = qc * 512 + h * 256
                        brel[7].acquire(*Ts)
                        for c8 in range(8):
                            ti = c8 % 2
                            xbt3_rel[ti].acquire(*Vs)
                            W(Vs, "dve", xr_dve[c8][qc])
                            cv = ctx.inc("dve", v.tensor_copy(
                                xbt3[ti], XR[:, c8 * T + col0: c8 * T + col0 + 256]))
                            W(Ts, "dve", cv)
                            mv = ctx.inc("pe", t.matmul(
                                bank(7)[0:1, 0:256], ones_bf[:, 0:1], xbt3[ti],
                                start=(c8 == 0), stop=(c8 == 7)))
                            xbt3_rel[ti].release("pe", mv)
                        for c8 in range(8):
                            ti = c8 % 2
                            xst3_rel[ti].acquire(*Ss)
                            W(Ss, "dve", xr_dve[c8][qc])
                            sv = ctx.inc("act", s.activation(
                                xst3[ti], XR[:, c8 * T + col0: c8 * T + col0 + 256],
                                AF.Square))
                            W(Ts, "act", sv)
                            mq = ctx.inc("pe", t.matmul(
                                bank(7)[0:1, 256:512], ones_bf[:, 0:1], xst3[ti],
                                start=(c8 == 0), stop=(c8 == 7)))
                            xst3_rel[ti].release("pe", mq)
                        W(Vs, "pe", mq)
                        dv1 = ctx.inc("dve", v.tensor_scalar_mul(
                            srow[0:1, 0:256], bank(7)[0:1, 0:256], 1.0 / D))
                        dv2 = ctx.inc("dve", v.tensor_scalar_mul(
                            srow[0:1, 512:768], bank(7)[0:1, 256:512], 1.0 / D))
                        W(Vs, "dve", dv2)
                        ctx.inc("dve", v.tensor_mul(srow[0:1, 1024:1280],
                                                    srow[0:1, 0:256], srow[0:1, 0:256]))
                        W(Vs, "dve", ctx.val("dve"))
                        dvv = ctx.inc("dve", v.tensor_sub(
                            srow[0:1, 1024:1280], srow[0:1, 512:768], srow[0:1, 1024:1280]))
                        W(Ss, "dve", dvv)
                        ctx.inc("act", s.activation(srow[0:1, 512:768], srow[0:1, 1024:1280],
                                                    AF.Ln, bias=eps_t[0:1, 0:1], scale=1.0))
                        W(Ss, "act", ctx.val("act"))
                        sv2 = ctx.inc("act", s.activation(srow[0:1, 512:768],
                                                          srow[0:1, 512:768],
                                                          AF.Exp, scale=-0.5))
                        W(Vs, "act", sv2)
                        dvn = ctx.inc("dve", v.scalar_tensor_tensor(
                            srow[0:1, 1024:1280], srow[0:1, 0:256], -1.0,
                            srow[0:1, 512:768], ALU.mult, ALU.mult))
                        W(Ts, "act", sv2)
                        W(Ts, "dve", dvn)
                        t.matmul(bank(7)[:, 0:256], ones_f, srow[0:1, 512:768],
                                 start=True, stop=True)
                        mvb = ctx.inc("pe", t.matmul(
                            bank(7)[:, 256:512], ones_f, srow[0:1, 1024:1280],
                            start=True, stop=True))
                        W(Vs, "pe", mvb)
                        lvh = 0
                        for c8 in range(8):
                            ni = c8 % 2
                            nrm3_rel[ni].acquire(*Vs)
                            dvt = ctx.inc("dve", v.tensor_tensor(
                                nrm3[ni], XR[:, c8 * T + col0: c8 * T + col0 + 256],
                                bank(7)[:, 0:256], ALU.mult))
                            W(Vs, "dve", dvt)
                            lvh = ctx.inc("dve", v.tensor_tensor(
                                ln2[:, c8 * T + col0: c8 * T + col0 + 256],
                                nrm3[ni], bank(7)[:, 256:512], ALU.add))
                            nrm3_rel[ni].release("dve", lvh)
                        brel[7].release("dve", lvh)
                        brel[7].acquire(*Ts)
                        W(Ts, "qsw", w_small)
                        W(Ts, "dve", lvh)
                        for c8 in range(8):
                            ins = t.matmul(bank(7)[0:7, 0:256],
                                           wr_b[:, c8 * 7:(c8 + 1) * 7],
                                           ln2[:, c8 * T + col0: c8 * T + col0 + 256],
                                           start=(c8 == 0), stop=(c8 == 7))
                        mlg = ctx.inc("pe", ins)
                        W(Ss, "pe", mlg)
                        slg = ctx.inc("act", s.activation(
                            lg[0:7, h * 256:(h + 1) * 256], bank(7)[0:7, 0:256], AF.Copy))
                        brel[7].release("act", slg)
                    ln2_done[qc] = lvh
                    brel[7].acquire(*Ts)
                    W(Ts, "act", slg)
                    W(Ts, "pool", ident_done)
                    for tt in range(4):
                        ins = t.transpose(PS[:, 7 * 512 + tt * 7: 7 * 512 + (tt + 1) * 7],
                                          lg[0:7, tt * 128:(tt + 1) * 128], ident[0:7, 0:7])
                    mtr = ctx.inc("pe", ins)
                    W(Vs, "pe", mtr)
                    dtr = ctx.inc("dve", v.tensor_copy(lgT, PS[:, 7 * 512: 7 * 512 + 28]))
                    brel[7].release("dve", dtr)

                    v3 = lambda a: a.rearrange("p (t j) -> p t j", j=7)
                    bc = lambda a: a.unsqueeze(2).broadcast_to([128, 4, 7])

                    def chain(ins):
                        r = ctx.inc("dve", ins)
                        W(Vs, "dve", r)
                        return r
                    W(Vs, "dve", dtr)
                    chain(v.reduce_max(st1, v3(lgT), axis=AX.X))
                    chain(v.tensor_tensor(v3(pr), v3(lgT), bc(st1), ALU.subtract))
                    W(Ss, "dve", ctx.val("dve"))
                    spr = ctx.inc("act", s.activation(pr, pr, AF.Exp))
                    W(Vs, "act", spr)
                    chain(v.reduce_sum(st2, v3(pr), axis=AX.X))
                    chain(v.reciprocal(st2, st2))
                    chain(v.tensor_tensor(v3(pr), v3(pr), bc(st2), ALU.mult))
                    chain(v.reduce_max(st1, v3(pr), axis=AX.X))
                    chain(v.tensor_tensor(v3(eq1), v3(pr), bc(st1), ALU.is_equal))
                    chain(v.tensor_scalar(eq2, eq1, -1.0, 1.0, ALU.mult, ALU.add))
                    chain(v.tensor_tensor(pr, pr, eq2, ALU.mult))
                    chain(v.reduce_max(st2, v3(pr), axis=AX.X))
                    chain(v.tensor_tensor(v3(eq2), v3(pr), bc(st2), ALU.is_equal))
                    chain(v.tensor_tensor(st3, st1, st2, ALU.add))
                    chain(v.tensor_scalar_add(st3, st3, 1e-9))
                    chain(v.reciprocal(st3, st3))
                    chain(v.tensor_tensor(st4, st1, st3, ALU.mult))
                    chain(v.tensor_tensor(st3, st2, st3, ALU.mult))
                    chain(v.tensor_tensor(v3(eq1), v3(eq1), bc(st4), ALU.mult))
                    chain(v.tensor_tensor(v3(eq2), v3(eq2), bc(st3), ALU.mult))
                    dcomb = ctx.inc("dve", v.tensor_tensor(combT, eq1, eq2, ALU.add))

                    brel[7].acquire(*Ts)
                    W(Ts, "dve", dcomb)
                    for tt in range(4):
                        ins = t.transpose(PS[0:7, 7 * 512 + tt * 128: 7 * 512 + (tt + 1) * 128],
                                          combT[:, tt * 7:(tt + 1) * 7], ident)
                    mtr2 = ctx.inc("pe", ins)
                    W(Vs, "pe", mtr2)
                    dcp = ctx.inc("dve", v.tensor_copy(
                        comb[0:7, qc * 512:(qc + 1) * 512], PS[0:7, 7 * 512:8 * 512]))
                    brel[7].release("dve", dcp)
                    comb_done[qc] = dcp

                for qc in range(QC):
                    for hp in range(NP):
                        qcol = slice(hp * T + qc * 512, hp * T + (qc + 1) * 512)
                        brel[4].acquire(*Ts)
                        brel[5].acquire(*Ts)
                        W(Ts, "dve", p1_dve)
                        W(Ts, "act", p1_act)
                        pv = 0
                        exp_of = {}

                        def issue_scores(kt):
                            sb = 2 * (kt % 2)
                            brel[sb].acquire(*Ts)
                            brel[sb + 1].acquire(*Ts)
                            t.matmul(bank(sb), K2[0:64, kt * 128:(kt + 1) * 128],
                                     Q[0:64, qcol], start=True, stop=True,
                                     tile_position=(0, 0))
                            ins = t.matmul(bank(sb + 1), K2[64:128, kt * 128:(kt + 1) * 128],
                                           Q[64:128, qcol], start=True, stop=True,
                                           tile_position=(64, 0))
                            mv = ctx.inc("pe", ins)
                            pt = kt % NPT
                            PT_rel[pt].acquire(*Ss)
                            W(Ss, "pe", mv)
                            W(Ss, "dve", bias_done)
                            ev = ctx.inc("act", s.activation(
                                PT[pt], PS[:, sb * 512: sb * 512 + 1024], AF.Exp,
                                bias=bias_hd[:, hp:hp + 1], scale=0.125))
                            brel[sb].release("act", ev)
                            brel[sb + 1].release("act", ev)
                            exp_of[kt] = ev

                        issue_scores(0)
                        issue_scores(1)
                        for kt in range(KT):
                            if kt + 2 < KT:
                                issue_scores(kt + 2)
                            pt = kt % NPT
                            W(Ts, "act", exp_of[kt])
                            t.matmul(bank(4), VA[:, kt * 65:kt * 65 + 128],
                                     PT[pt][:, 0:512], start=(kt == 0), stop=(kt == KT - 1))
                            pv = ctx.inc("pe", t.matmul(
                                bank(5), VA[:, kt * 65:kt * 65 + 128],
                                PT[pt][:, 512:1024], start=(kt == 0), stop=(kt == KT - 1)))
                            PT_rel[pt].release("pe", pv)
                        rd_rel.acquire(*Ss)
                        W(Ss, "pe", pv)
                        s.activation(rdrow2[0:1, 0:512], PS[64:65, 4 * 512:5 * 512], AF.Ln)
                        lnv = ctx.inc("act", s.activation(
                            rdrow2[0:1, 512:1024], PS[64:65, 5 * 512:6 * 512], AF.Ln))
                        W(Ss, "act", lnv)
                        rde = ctx.inc("act", s.activation(rdrow2, rdrow2, AF.Exp, scale=-1.0))
                        brel[6].acquire(*Ts)
                        W(Ts, "act", rde)
                        t.matmul(PS[0:64, 6 * 512:7 * 512], ones_f[0:1, 0:64],
                                 rdrow2[0:1, 0:512], start=True, stop=True)
                        mb = ctx.inc("pe", t.matmul(
                            PS[64:128, 6 * 512:7 * 512], ones_f[0:1, 0:64],
                            rdrow2[0:1, 512:1024], start=True, stop=True))
                        rd_rel.release("pe", mb)
                        a2 = 0
                        for hh in range(2):
                            rows = slice(64 * hh, 64 * hh + 64)
                            ab = 4 + hh
                            W(Ss, "dve", a2_prev[0] if hh == 0 else a2)
                            sc = ctx.inc("act", s.activation(
                                att_s[rows, :], PS[0:64, ab * 512:(ab + 1) * 512], AF.Copy))
                            brel[ab].release("act", sc)
                            W(Vs, "act", sc)
                            W(Vs, "pe", mb)
                            m1 = ctx.inc("dve", v.tensor_tensor(
                                att_s[rows, :], att_s[rows, :],
                                PS[rows, 6 * 512:7 * 512], ALU.mult))
                            W(Vs, "qxr", xr_load)
                            a2 = ctx.inc("dve", v.tensor_tensor(
                                XR[rows, hp * T + qc * 512: hp * T + (qc + 1) * 512],
                                XR[rows, hp * T + qc * 512: hp * T + (qc + 1) * 512],
                                att_s[rows, :], ALU.add))
                        brel[6].release("dve", a2)
                        a2_prev[0] = a2
                        xr_dve[hp][qc] = a2
                        if hp == 1 and qc > 0:
                            p3_chunk(qc - 1)
                p3_chunk(QC - 1)

                tap("X2", [128, 8 * T], XR, "dve")
                tap("ln2", [128, 4 * T], ln2.bitcast(F32), "dve")
                tap("comb", [7, T], comb, "dve")
                ctx.barrier(ALL_ENG, exclude=("qpc",))
            ctx.barrier(ALL_ENG, exclude=("qpc",))

        with ExitStack() as fb:
            fb.enter_context(nc.named_scope("P4"))
            SB4 = lambda n, sh, dt=F32: fb.enter_context(nc.sbuf_tensor(n, sh, dt)).ap()
            silg = SB4("silg", [128, T])
            silg_rel = Rel(ctx)
            hsb = SB4("hsb", [128, 8 * T], BF16)
            hsb_rel = [Rel(ctx) for _ in range(8)]
            cbs = SB4("cbs", [128, T])
            cbs_rel = Rel(ctx)
            crow = SB4("crow", [1, T])
            crow_rel = Rel(ctx)
            wgb = [SB4(f"wgb{i}", [128, 8 * 128], BF16) for i in range(2)]
            wgb_rel = [Rel(ctx) for _ in range(2)]
            wub = [SB4(f"wub{i}", [128, 8 * 128], BF16) for i in range(2)]
            wub_rel = [Rel(ctx) for _ in range(2)]
            wdb = [SB4(f"wdb{i}", [128, 8 * 128], BF16) for i in range(2)]
            wdb_rel = [Rel(ctx) for _ in range(2)]

            dcp4 = comb_done[QC - 1]
            xr_last = [[xr_dve[oc][qc] for qc in range(QC)] for oc in range(8)]
            ln2_all = max(ln2_done)
            for e in range(8):
                if e > 0:
                    crow_rel.acquire(*Qs)
                    W(Qs, "dve", dcp4)
                    cr = ctx.inc("qg", sync.dma_start(crow, comb[e - 1:e, :]))
                    cbs_rel.acquire(*Vs)
                    for qc in range(QC):
                        bk = 2 + qc % 2
                        brel[bk].acquire(*Ts)
                        W(Ts, "qg", cr)
                        ins = t.matmul(bank(bk), ones_f,
                                       crow[0:1, qc * 512:(qc + 1) * 512],
                                       start=True, stop=True)
                        mv = ctx.inc("pe", ins)
                        W(Vs, "pe", mv)
                        dv = ctx.inc("dve", v.tensor_copy(
                            cbs[:, qc * 512:(qc + 1) * 512], bank(bk)))
                        brel[bk].release("dve", dv)
                    crow_rel.release("pe", ctx.val("pe"))
                    cb_done = ctx.val("dve")
                else:
                    cb_done = 0

                for db in range(4):
                    h_done = 0
                    for pc in range(8):
                        wi = pc % 2
                        wgb_rel[wi].acquire(*Qs)
                        W(Qs, "qpc", 48 if e == 0 else qpc_all)
                        gv = ctx.inc(f"qw{wi}", sync.dma_start(wgb[wi], gate_b[e, db, pc]))
                        wub_rel[wi].acquire(*Qs)
                        uv = ctx.inc(f"qw{2 + wi}", sync.dma_start(wub[wi], up_b[e, db, pc]))
                        for qc in range(QC):
                            brel[qc].acquire(*Ts)
                        W(Ts, f"qw{wi}", gv)
                        W(Ts, "dve", ln2_all)
                        ins = None
                        for qc in range(QC):
                            for c8 in range(8):
                                ins = t.matmul(
                                    bank(qc), wgb[wi][:, c8 * 128:(c8 + 1) * 128],
                                    ln2[:, c8 * T + qc * 512: c8 * T + (qc + 1) * 512],
                                    start=(c8 == 0), stop=(c8 == 7))
                        mg = ctx.inc("pe", ins)
                        for qc in range(QC):
                            brel[4 + qc].acquire(*Ts)
                        W(Ts, f"qw{2 + wi}", uv)
                        for qc in range(QC):
                            for c8 in range(8):
                                ins = t.matmul(
                                    bank(4 + qc), wub[wi][:, c8 * 128:(c8 + 1) * 128],
                                    ln2[:, c8 * T + qc * 512: c8 * T + (qc + 1) * 512],
                                    start=(c8 == 0), stop=(c8 == 7))
                        mu = ctx.inc("pe", ins)
                        wgb_rel[wi].release("pe", mg)
                        wub_rel[wi].release("pe", mu)
                        silg_rel.acquire(*Ss)
                        W(Ss, "pe", mg)
                        ssg = ctx.inc("act", s.activation(silg, PS[:, 0:2048], AF.Silu))
                        for qc in range(QC):
                            brel[qc].release("act", ssg)
                        hsb_rel[pc].acquire(*Vs)
                        W(Vs, "act", ssg)
                        W(Vs, "pe", mu)
                        if e > 0:
                            W(Vs, "dve", cb_done)
                            ctx.inc("dve", v.tensor_tensor(silg, PS[:, 2048:4096],
                                                           silg, ALU.mult))
                            W(Vs, "dve", ctx.val("dve"))
                            hv = ctx.inc("dve", v.tensor_tensor(
                                hsb[:, pc * T:(pc + 1) * T], silg, cbs, ALU.mult))
                        else:
                            hv = ctx.inc("dve", v.tensor_tensor(
                                hsb[:, pc * T:(pc + 1) * T], PS[:, 2048:4096],
                                silg, ALU.mult))
                        silg_rel.release("dve", hv)
                        for qc in range(QC):
                            brel[4 + qc].release("dve", hv)
                        h_done = hv
                    for oc in range(8):
                        wi = oc % 2
                        wdb_rel[wi].acquire(*Qs)
                        dv_w = ctx.inc(f"qw{4 + wi}", sync.dma_start(wdb[wi], down_b[e, db, oc]))
                        for qc in range(QC):
                            bk = qc
                            brel[bk].acquire(*Ts)
                            W(Ts, f"qw{4 + wi}", dv_w)
                            W(Ts, "dve", h_done)
                            ins = None
                            for pcx in range(8):
                                ins = t.matmul(
                                    bank(bk), wdb[wi][:, pcx * 128:(pcx + 1) * 128],
                                    hsb[:, pcx * T + qc * 512: pcx * T + (qc + 1) * 512],
                                    start=(pcx == 0), stop=(pcx == 7))
                            md = ctx.inc("pe", ins)
                            W(Vs, "pe", md)
                            W(Vs, "dve", xr_last[oc][qc])
                            dvx = ctx.inc("dve", v.tensor_tensor(
                                XR[:, oc * T + qc * 512: oc * T + (qc + 1) * 512],
                                XR[:, oc * T + qc * 512: oc * T + (qc + 1) * 512],
                                bank(bk), ALU.add))
                            xr_last[oc][qc] = dvx
                            brel[bk].release("dve", dvx)
                            if e == 7 and db == 3:
                                W(Qs, "dve", dvx)
                                ctx.inc("st", sync.dma_start(
                                    out[oc * 128:(oc + 1) * 128, qc * 512:(qc + 1) * 512],
                                    XR[:, oc * T + qc * 512: oc * T + (qc + 1) * 512]))
                        wdb_rel[wi].release("pe", ctx.val("pe"))
                    pe_after_down = ctx.val("pe")
                    for pc in range(8):
                        hsb_rel[pc].release("pe", pe_after_down)
                cbs_rel.release("dve", ctx.val("dve"))

            sync.wait_ge(ctx.sems["st"][0], ctx.val("st"))
    return nc, tap_aps



def host_inputs(inputs, core):
    b, half = core // 2, core % 2
    x = np.asarray(inputs["L0_attn_x"])
    xT = np.ascontiguousarray(x[b].T)
    x_own = np.ascontiguousarray(xT[:, half * T:(half + 1) * T])
    x_oth = np.ascontiguousarray(xT[:, (1 - half) * T:(2 - half) * T])
    x_perm = np.ascontiguousarray(np.concatenate([x_oth, x_own], axis=1))

    def tile_in_by_out(w):
        wt = np.ascontiguousarray(np.asarray(w).T)
        o = w.shape[0]
        return np.ascontiguousarray(
            wt.reshape(8, 128, o).transpose(1, 0, 2).reshape(128, 8 * o))

    wq_h = tile_in_by_out(inputs["L0_attn_Wq"])
    wk_h = tile_in_by_out(inputs["L0_attn_Wk"])
    wv_h = tile_in_by_out(inputs["L0_attn_Wv"])
    wr_h = tile_in_by_out(inputs["L0_ffn_router_w"])

    def tile_gateup(w):
        wt = np.asarray(w).T
        r = wt.reshape(8, 128, 4, 8, 128)
        return np.ascontiguousarray(r.transpose(2, 3, 1, 0, 4).reshape(4, 8, 128, 1024))

    def tile_down(w):
        wt = np.asarray(w).T
        r = wt.reshape(4, 8, 128, 8, 128)
        return np.ascontiguousarray(r.transpose(0, 3, 2, 1, 4).reshape(4, 8, 128, 1024))

    gate_h = np.stack([tile_gateup(inputs["L0_ffn_shared_gate"])]
                      + [tile_gateup(np.asarray(inputs["L0_ffn_routed_gate"])[i])
                         for i in range(7)])
    up_h = np.stack([tile_gateup(inputs["L0_ffn_shared_up"])]
                    + [tile_gateup(np.asarray(inputs["L0_ffn_routed_up"])[i])
                       for i in range(7)])
    down_h = np.stack([tile_down(inputs["L0_ffn_shared_down"])]
                      + [tile_down(np.asarray(inputs["L0_ffn_routed_down"])[i])
                         for i in range(7)])

    return dict(x_full=x_perm, x_own=x_own, wq=wq_h, wk=wk_h, wv=wv_h, wr=wr_h,
                gate_t=gate_h, up_t=up_h, down_t=down_h)


def kernel(**inputs):
    nc, _ = build()
    shared = None
    in_maps = []
    for c in range(8):
        m = host_inputs(inputs, c)
        if shared is None:
            shared = {k: m[k] for k in ("wq", "wk", "wv", "wr", "gate_t", "up_t", "down_t")}
        else:
            for k in shared:
                m[k] = shared[k]
        in_maps.append(m)
    res = run_bass_kernel_spmd(nc, in_maps, core_ids=list(range(8)))
    outT = np.zeros((4, D, S), dtype=np.float32)
    for c in range(8):
        b, half = c // 2, c % 2
        outT[b][:, half * T:(half + 1) * T] = res.results[c]["out"]
    return np.ascontiguousarray(outT.transpose(0, 2, 1))
